# version 09
# baseline (speedup 1.0000x reference)
import numpy as np
from contextlib import ExitStack

import concourse.bass as bass
import concourse.mybir as mybir
from concourse.bass_utils import run_bass_kernel_spmd
from concourse.bass_isa_rust import ReduceOp

F32 = mybir.dt.float32
BF16 = mybir.dt.bfloat16
I32 = mybir.dt.int32
ALU = mybir.AluOpType
ACT = mybir.ActivationFunctionType
AXX = mybir.AxisListType.X

T = 2048
D = 1024
DI = 1536
DS = 8
I3 = 3072
II = 4096
NT = T // 128
NG = 4
GT = 512
NCORE = 8
LN2C = float(np.log(2.0))
LN6C = float(np.log(6.0))


def build_nc():
    nc = bass.Bass("TRN2", target_bir_lowering=False)

    x_d = nc.dram_tensor("x", [T, D], F32, kind="ExternalInput")
    winT_d = nc.dram_tensor("winT", [D, I3], F32, kind="ExternalInput")
    woutT_d = nc.dram_tensor("woutT", [DI, D], F32, kind="ExternalInput")
    gslice_d = nc.dram_tensor("gate_slice", [II // 8, D], F32, kind="ExternalInput")
    uslice_d = nc.dram_tensor("up_slice", [II // 8, D], F32, kind="ExternalInput")
    dslice_d = nc.dram_tensor("down_slice", [D // 8, II], F32, kind="ExternalInput")
    alog_d = nc.dram_tensor("a_log", [DI, DS], F32, kind="ExternalInput")
    bmat_d = nc.dram_tensor("b_mat", [DI, DS], F32, kind="ExternalInput")
    cmat_d = nc.dram_tensor("c_mat", [DI, DS], F32, kind="ExternalInput")
    lng_d = nc.dram_tensor("ln_in_g", [1, D], F32, kind="ExternalInput")
    lnb_d = nc.dram_tensor("ln_in_b", [1, D], F32, kind="ExternalInput")
    lng2_d = nc.dram_tensor("ln_out_g", [1, D], F32, kind="ExternalInput")
    lnb2_d = nc.dram_tensor("ln_out_b", [1, D], F32, kind="ExternalInput")
    sin_d = nc.dram_tensor("s_in", [1, 1], F32, kind="ExternalInput")
    sind_d = nc.dram_tensor("s_in_down", [1, 1], F32, kind="ExternalInput")
    out_d = nc.dram_tensor("out", [T, D], F32, kind="ExternalOutput")

    xq_dd = nc.dram_tensor("xq_dd", [T, D], BF16, kind="Internal")
    xn2_dd = nc.dram_tensor("xn2_dd", [T, D], BF16, kind="Internal")
    s_dd = nc.dram_tensor("s_dd", [T, D], F32, kind="Internal")
    hq_dd = nc.dram_tensor("hq_dd", [T, II], BF16, kind="Internal")
    bias_dd = nc.dram_tensor("bias_dd", [1, I3], F32, kind="Internal")
    wsg_sl = nc.dram_tensor("wsg_sl", [II // 8, D], BF16, kind="Internal")
    wsu_sl = nc.dram_tensor("wsu_sl", [II // 8, D], BF16, kind="Internal")
    wsd_sl = nc.dram_tensor("wsd_sl", [D // 8, II], BF16, kind="Internal")
    wsg_f = nc.dram_tensor("wsg_f", [II, D], BF16, kind="Internal")
    wsu_f = nc.dram_tensor("wsu_f", [II, D], BF16, kind="Internal")
    wsd_f = nc.dram_tensor("wsd_f", [D, II], BF16, kind="Internal")
    am8_in = nc.dram_tensor("am8_in", [1, 8], F32, kind="Internal")
    am8_out = nc.dram_tensor("am8_out", [1, 8], F32, kind="Internal")
    gmx_dd = nc.dram_tensor("gmx_dd", [128, 4], F32, kind="Internal")

    rg = [list(range(NCORE))]
    VE, SE, TE, GE, SY = nc.vector, nc.scalar, nc.tensor, nc.gpsimd, nc.sync

    st = ExitStack()
    nc._keep = st

    def sbuf(name, shape, dt=F32, stk=None):
        return (stk or st).enter_context(nc.sbuf_tensor(name, shape, dt))

    class Sem:
        def __init__(self, name):
            self.h = st.enter_context(nc.semaphore(name))
            self.n = 0

        def inc(self, instr, k=1):
            self.n += k
            instr.then_inc(self.h, k)

        def d(self, instr):
            self.inc(instr, 16)

        def wt(self, eng, n=None):
            eng.wait_ge(self.h, self.n if n is None else n)

    s_ld = Sem("s_ld")
    s_sc = Sem("s_sc")
    s_gd = Sem("s_gd")
    s_st = Sem("s_st")
    s_v = Sem("s_v")
    s_s = Sem("s_s")
    s_t = Sem("s_t")
    s_g = Sem("s_g")
    ALLSEMS = [s_ld, s_sc, s_gd, s_st, s_v, s_s, s_t, s_g]

    def vsync():
        s_v.wt(VE)

    def barrier():
        for eng in (VE, SE, TE, GE, SY):
            for sm in ALLSEMS:
                if sm.n:
                    sm.wt(eng)

    C = sbuf("smalls", [128, 64])
    decay_t = sbuf("decay_t", [128, 96])
    bpos_t = sbuf("bpos_t", [128, 96])
    bneg_t = sbuf("bneg_t", [128, 96])
    wyc_t = sbuf("wyc_t", [128, 96])
    carry_t = sbuf("carry_t", [128, 96])
    g2rep = sbuf("g2rep", [128, D])
    b2rep = sbuf("b2rep", [128, D])
    bias_col = sbuf("bias_col", [128, 24])
    gcol_t = sbuf("gcol_t", [128, 8])
    bcol_t = sbuf("bcol_t", [128, 8])
    mv6 = sbuf("mv6", [128, 12])
    mv2 = sbuf("mv2", [128, 2])

    def col(i):
        return C[:, i:i + 1]

    _cc = {}

    def ccol(val):
        val = float(val)
        if val not in _cc:
            idx = 32 + len(_cc)
            _cc[val] = idx
            s_v.inc(VE.memset(col(idx), val))
        return col(_cc[val])

    for _v in (1.0, -1.0, -LN6C, 3.0 * LN2C, -LN2C, -4.0 * LN2C, LN2C):
        ccol(_v)

    def ln_stats(src, cr, cmr):
        s_v.inc(VE.bn_stats(mv6[:, 0:6], src[:, 0:512]))
        s_v.inc(VE.bn_stats(mv6[:, 6:12], src[:, 512:1024]))
        vsync()
        s_v.inc(VE.bn_aggr(mv2[:], mv6[:]))
        vsync()
        s_v.inc(VE.tensor_scalar(col(16), mv2[:, 1:2], 1e-5, None, ALU.add))
        s_v.wt(SE)
        s_s.inc(SE.activation(col(17), col(16), ACT.Sqrt))
        s_s.wt(VE)
        s_v.inc(VE.reciprocal(col(18), col(17)))
        vsync()
        s_v.inc(VE.tensor_tensor(col(19), col(18), col(18), ALU.mult))
        vsync()
        s_v.inc(VE.tensor_tensor(col(19), col(16), col(19), ALU.mult))
        vsync()
        s_v.inc(VE.tensor_scalar(col(19), col(19), -0.5, 1.5, ALU.mult, ALU.add))
        vsync()
        s_v.inc(VE.tensor_tensor(col(cr), col(18), col(19), ALU.mult))
        vsync()
        s_v.inc(VE.tensor_tensor(col(18), mv2[:, 0:1], col(cr), ALU.mult))
        vsync()
        s_v.inc(VE.tensor_scalar(col(cmr), col(18), -1.0, None, ALU.mult))
        vsync()

    def make_qbufs(stk, tag):
        return dict(
            amax=sbuf(f"amax{tag}", [128, 1024], F32, stk),
            sc8h=sbuf(f"sc8h{tag}", [128, 1024], BF16, stk),
            grep=sbuf(f"grep{tag}", [128, 1024], F32, stk),
            t1=sbuf(f"qt1{tag}", [128, 1024], F32, stk),
            t2=sbuf(f"qt2{tag}", [128, 1024], F32, stk),
            ti=sbuf(f"qti{tag}", [128, 1024], I32, stk),
            z=sbuf(f"qz{tag}", [128, 1024], F32, stk),
            az=sbuf(f"qaz{tag}", [128, 1024], F32, stk),
            L=sbuf(f"qL{tag}", [128, 1024], I32, stk),
            G=sbuf(f"qG{tag}", [128, 1024], I32, stk),
            F=sbuf(f"qF{tag}", [128, 1024], I32, stk),
            U=sbuf(f"qU{tag}", [128, 1024], F32, stk),
            Q=sbuf(f"qQ{tag}", [128, 1024], BF16, stk),
            S=sbuf(f"qS{tag}", [128, 1024], BF16, stk),
            X=sbuf(f"qX{tag}", [128, 1024], BF16, stk),
        )

    def scale_chain(qb, Fb, gs_col, rgs_col):
        amax, sc8h, grep = qb["amax"], qb["sc8h"], qb["grep"]
        t1, t2, ti = qb["t1"], qb["t2"], qb["ti"]
        s_v.inc(VE.tensor_scalar(col(19), C[:, gs_col:gs_col + 1], 1.0 / 6.0, None,
                                 ALU.mult))
        vsync()
        s_v.inc(VE.tensor_scalar(t1[:, 0:Fb], amax[:, 0:Fb], col(19), 448.0,
                                 ALU.mult, ALU.min))
        s_v.wt(SE)
        s_s.inc(SE.activation(t2[:, 0:Fb], t1[:, 0:Fb], ACT.Ln))
        s_s.wt(VE)
        s_v.inc(VE.tensor_scalar(t2[:, 0:Fb], t2[:, 0:Fb], 1.0 / LN2C, -0.5,
                                 ALU.mult, ALU.add))
        s_v.inc(VE.tensor_scalar(ti[:, 0:Fb], t2[:, 0:Fb], -6.0, None, ALU.max))
        e8i = qb["Q"]
        e8b = qb["S"]
        s_v.wt(SE)
        s_s.inc(SE.activation(e8i[:, 0:Fb], ti[:, 0:Fb], ACT.Exp,
                              bias=ccol(3.0 * LN2C), scale=ccol(-LN2C)))
        s_s.inc(SE.activation(e8b[:, 0:Fb], ti[:, 0:Fb], ACT.Exp,
                              bias=ccol(-4.0 * LN2C), scale=ccol(LN2C)))
        s_s.wt(VE)
        s_v.inc(VE.tensor_tensor(t1[:, 0:Fb], t1[:, 0:Fb], e8i[:, 0:Fb], ALU.mult))
        s_v.inc(VE.tensor_scalar(ti[:, 0:Fb], t1[:, 0:Fb], 1.0, None, ALU.mult))
        s_v.inc(VE.tensor_tensor(sc8h[:, 0:Fb], ti[:, 0:Fb], e8b[:, 0:Fb], ALU.mult))
        s_v.inc(VE.tensor_scalar(t2[:, 0:Fb], sc8h[:, 0:Fb], 2.0, None, ALU.mult))
        s_v.inc(VE.tensor_scalar(t2[:, 0:Fb], t2[:, 0:Fb],
                                 C[:, rgs_col:rgs_col + 1], 1e-30, ALU.mult, ALU.max))
        s_v.inc(VE.reciprocal(grep[:, 0:Fb], t2[:, 0:Fb]))

    def quant_values(qb, src, blk_off, Fn, dst_ap, store_sem):
        nb = Fn // 16
        grepap = qb["grep"][:, blk_off:blk_off + nb].rearrange(
            "p (b o) -> p b o", o=1).broadcast_to([128, nb, 16])
        sc8ap = qb["sc8h"][:, blk_off:blk_off + nb].rearrange(
            "p (b o) -> p b o", o=1).broadcast_to([128, nb, 16])
        z, az = qb["z"][:, 0:Fn], qb["az"][:, 0:Fn]
        L, G, F = qb["L"][:, 0:Fn], qb["G"][:, 0:Fn], qb["F"][:, 0:Fn]
        U, Q, Sg, Xs = qb["U"][:, 0:Fn], qb["Q"][:, 0:Fn], qb["S"][:, 0:Fn], qb["X"][:, 0:Fn]
        s_s.wt(VE)
        s_v.inc(VE.tensor_tensor(z, src, grepap, ALU.mult))
        s_v.wt(SE)
        s_s.inc(SE.activation(az, z, ACT.Abs))
        s_s.inc(SE.activation(Sg, z, ACT.Sign))
        t1 = qb["t1"][:, 0:Fn]
        t2 = qb["t2"][:, 0:Fn]
        s_s.wt(VE)
        s_v.inc(VE.tensor_scalar(t1, az, 1.75, None, ALU.is_ge))
        s_v.inc(VE.scalar_tensor_tensor(t2, az, 1.25, t1, ALU.is_ge, ALU.add))
        s_v.inc(VE.scalar_tensor_tensor(t1, az, 0.75, t2, ALU.is_ge, ALU.add))
        s_v.inc(VE.scalar_tensor_tensor(t2, az, 0.25, t1, ALU.is_ge, ALU.add))
        s_v.inc(VE.tensor_scalar(U, az, 1.9999999, 0.0, ALU.subtract, ALU.max))
        s_v.inc(VE.tensor_scalar(G, U, 2.0, None, ALU.min))
        s_v.inc(VE.tensor_scalar(U, az, 4.4999995, 0.0, ALU.subtract, ALU.max))
        s_v.inc(VE.tensor_scalar(F, U, 1.0, None, ALU.min))
        s_v.inc(VE.scalar_tensor_tensor(U, G, 2.0, t2, ALU.mult, ALU.add))
        s_v.inc(VE.scalar_tensor_tensor(Q, F, 4.0, U, ALU.mult, ALU.add))
        s_v.inc(VE.tensor_tensor(U, Q, sc8ap, ALU.mult))
        store_sem.wt(VE)
        s_v.inc(VE.tensor_tensor(Xs, U, Sg, ALU.mult))
        s_v.wt(GE)
        store_sem.d(GE.dma_start(dst_ap, Xs))

    s_ld.d(SY.dma_start(decay_t[:].rearrange("p (j d) -> p j d", d=8), alog_d[:, :].rearrange("(j p) d -> p j d", p=128)))
    s_ld.d(SY.dma_start(bpos_t[:].rearrange("p (j d) -> p j d", d=8), bmat_d[:, :].rearrange("(j p) d -> p j d", p=128)))
    s_ld.d(SY.dma_start(wyc_t[:].rearrange("p (j d) -> p j d", d=8), cmat_d[:, :].rearrange("(j p) d -> p j d", p=128)))
    with nc.allow_non_contiguous_dma(reason="tiny per-k-tile gain/bias column loads"):
        s_ld.d(SY.dma_start(gcol_t[:], lng_d[0:1, :].rearrange("o (j p) -> p o j", p=128).squeeze(1)))
        s_ld.d(SY.dma_start(bcol_t[:], lnb_d[0:1, :].rearrange("o (j p) -> p o j", p=128).squeeze(1)))
    with nc.allow_non_contiguous_dma(reason="scalar broadcasts"):
        s_ld.d(SY.dma_start(C[:, 0:1], sin_d[0:1, 0:1].broadcast_to([128, 1])))
        s_ld.d(SY.dma_start(C[:, 1:2], sind_d[0:1, 0:1].broadcast_to([128, 1])))
    s_ld.d(SY.dma_start(g2rep[:], lng2_d[0:1, :].broadcast_to([128, D])))
    s_ld.d(SY.dma_start(b2rep[:], lnb2_d[0:1, :].broadcast_to([128, D])))
    ld0 = s_ld.n
    s_ld.wt(SE, ld0)
    s_s.inc(SE.activation(col(2), col(0), ACT.Ln))
    s_s.inc(SE.activation(col(3), col(1), ACT.Ln))
    s_ld.wt(VE, ld0)
    s_v.inc(VE.reciprocal(col(26), col(0)))
    s_v.inc(VE.reciprocal(col(27), col(1)))
    vsync()

    s_ld.wt(VE, ld0)
    s_v.inc(VE.tensor_scalar(carry_t[:], decay_t[:], 0.0, -5.0, ALU.min, ALU.max))
    s_v.wt(SE)
    s_s.inc(SE.activation(carry_t[:], carry_t[:], ACT.Exp))
    s_s.wt(VE)
    s_v.inc(VE.tensor_scalar(carry_t[:], carry_t[:], 0.01, 2.0, ALU.max, ALU.min))
    vsync()
    s_v.inc(VE.tensor_scalar(decay_t[:], carry_t[:], -0.9, None, ALU.mult))
    s_v.inc(VE.tensor_tensor(carry_t[:], bpos_t[:], wyc_t[:], ALU.mult))
    vsync()
    s_v.inc(VE.tensor_scalar(wyc_t[:], carry_t[:], 0.1, None, ALU.mult))
    s_v.wt(SE)
    s_s.inc(SE.activation(carry_t[:], bpos_t[:], ACT.Abs))
    s_s.wt(VE)
    s_v.inc(VE.tensor_scalar(carry_t[:], carry_t[:], 1e-30, None, ALU.max))
    vsync()
    s_v.inc(VE.reciprocal(bpos_t[:], carry_t[:]))
    vsync()
    s_v.inc(VE.tensor_scalar(bpos_t[:], bpos_t[:], 50.0, None, ALU.mult))
    vsync()
    s_v.inc(VE.tensor_scalar(bneg_t[:], bpos_t[:], -1.0, None, ALU.mult))
    s_v.inc(VE.memset(carry_t[:], 0.0))

    with ExitStack() as sk:
        sxn = sbuf("sxn", [128, NT * D], F32, sk)
        qb = make_qbufs(sk, "a")
        s_ld.d(SY.dma_start(sxn[:].rearrange("p (n d) -> p n d", d=D), x_d[:, :].rearrange("(n p) d -> p n d", p=128)))
        ldx = s_ld.n
        s_ld.wt(VE, ldx)
        s_ld.wt(SE, ldx)
        for n in range(NT):
            xt = sxn[:, n * D:(n + 1) * D]
            ln_stats(xt, 20, 21)
            s_v.wt(SE)
            s_s.inc(SE.activation(xt, xt, ACT.Identity, bias=col(21), scale=col(20)))
            s_s.wt(VE)
        for n in range(NT):
            xt = sxn[:, n * D:(n + 1) * D]
            ln_stats(xt, 20, 21)
            s_v.wt(SE)
            s_gd.wt(SE)
            s_s.inc(SE.activation(qb["X"][:, 0:D], xt, ACT.Identity, bias=col(21),
                                  scale=col(20)))
            s_s.wt(GE)
            s_gd.d(GE.dma_start(xn2_dd[n * 128:(n + 1) * 128, :], qb["X"][:, 0:D]))
            xb = xt.rearrange("p (b k) -> p b k", k=16)
            s_v.inc(VE.tensor_reduce(qb["amax"][:, n * 64:(n + 1) * 64], xb, AXX,
                                     ALU.max, apply_absolute_value=True))
        scale_chain(qb, NT * 64, 0, 26)
        for n in range(NT):
            quant_values(qb, sxn[:, n * D:(n + 1) * D], n * 64, D,
                         xq_dd[n * 128:(n + 1) * 128, :], s_gd)
        barrier()

    ska = ExitStack()
    winT_bf = sbuf("winT_bf", [128, 8 * I3], BF16, ska)
    woutT_bf = sbuf("woutT_bf", [128, 12 * D], BF16, ska)
    with ExitStack() as sk:
        wtmp = sbuf("wtmp", [128, 2 * I3], F32, sk)
        row1 = sbuf("row1", [1, I3], F32, sk)
        pbias = [sk.enter_context(nc.psum_tensor(f"pbias{i}", [1, 512], F32)) for i in range(6)]
        s_gd.d(GE.dma_start(woutT_bf[:].rearrange("p (j d) -> p j d", d=D),
                            woutT_d[:, :].rearrange("(j p) d -> p j d", p=128)))
        for kj in range(8):
            buf = wtmp[:, (kj % 2) * I3:(kj % 2) * I3 + I3]
            if kj >= 2:
                s_s.wt(SY)
                s_t.wt(SY)
            s_ld.d(SY.dma_start(buf, winT_d[kj * 128:(kj + 1) * 128, :]))
            s_ld.wt(SE)
            s_s.inc(SE.activation(winT_bf[:, kj * I3:(kj + 1) * I3], buf, ACT.Copy,
                                  scale=gcol_t[:, kj:kj + 1]))
            s_ld.wt(TE)
            for c in range(6):
                ins = TE.matmul(pbias[c][:], bcol_t[:, kj:kj + 1],
                                buf[:, c * 512:(c + 1) * 512],
                                start=(kj == 0), stop=(kj == 7))
            s_t.inc(ins)
        s_t.wt(VE)
        for c in range(6):
            s_v.inc(VE.tensor_copy(row1[0:1, c * 512:(c + 1) * 512], pbias[c][:]))
        s_v.wt(GE)
        s_gd.d(GE.dma_start(bias_dd[0:1, :], row1[:]))
        s_gd.wt(SY)
        with nc.allow_non_contiguous_dma(reason="tiny bias column reload"):
            s_ld.d(SY.dma_start(bias_col[:, 0:24],
                                bias_dd[0:1, :].rearrange("o (c p) -> p o c", p=128).squeeze(1)))
        barrier()

    with ExitStack() as sk:
        wq = sbuf("wq", [128, 4 * D], F32, sk)
        qb = make_qbufs(sk, "w")
        srcs = [(gslice_d, 1), (uslice_d, 1), (dslice_d, None)]
        for wi, (src_d, pat) in enumerate(srcs):
            s_v.wt(SY)
            if pat:
                s_ld.d(SY.dma_start(wq[:].rearrange("p (j d) -> p j d", d=D),
                                    src_d[:, :].rearrange("(j p) d -> p j d", p=128)))
            else:
                s_ld.d(SY.dma_start(wq[:], src_d[:, :]))
            s_ld.wt(VE)
            wb = wq[:].rearrange("p (b k) -> p b k", k=16)
            s_v.inc(VE.tensor_reduce(qb["amax"][:, 0:256], wb, AXX, ALU.max,
                                     apply_absolute_value=True))
            s_v.inc(VE.tensor_reduce(col(4 + wi), qb["amax"][:, 0:256], AXX, ALU.max))
        s_v.wt(GE)
        s_gd.d(GE.dma_start(gmx_dd[:, :], C[:, 4:8]))
        s_gd.wt(SY)
        with nc.allow_non_contiguous_dma(reason="gmax transpose readback"):
            for wi in range(3):
                s_ld.d(SY.dma_start(qb["t1"][0:1, wi * 128:(wi + 1) * 128],
                                    gmx_dd[:, wi:wi + 1].transpose([1, 0])))
        s_ld.wt(VE)
        s_v.inc(VE.memset(qb["t2"][0:1, 0:8], 0.0))
        for wi in range(3):
            s_v.inc(VE.tensor_reduce(qb["t2"][0:1, wi:wi + 1],
                                     qb["t1"][0:1, wi * 128:(wi + 1) * 128],
                                     AXX, ALU.max))
        s_v.wt(GE)
        s_gd.d(GE.dma_start(am8_in[0:1, :], qb["t2"][0:1, 0:8]))
        s_gd.wt(GE)
        s_g.inc(GE.collective_compute("AllReduce", ALU.max, rg,
                                      [am8_in[:, :]], [am8_out[:, :]]))
        s_g.wt(SY)
        with nc.allow_non_contiguous_dma(reason="gmax broadcast"):
            for wi in range(3):
                s_ld.d(SY.dma_start(C[:, 4 + wi:5 + wi],
                                    am8_out[0:1, wi:wi + 1].broadcast_to([128, 1])))
        s_ld.wt(VE)
        for wi in range(3):
            s_v.inc(VE.reciprocal(col(20 + wi), col(4 + wi)))
        vsync()
        for wi in range(3):
            s_v.inc(VE.tensor_scalar(col(7 + wi), col(20 + wi), 2688.0, None, ALU.mult))
        vsync()
        s_v.wt(SE)
        for wi in range(3):
            s_s.inc(SE.activation(col(10 + wi), col(7 + wi), ACT.Ln))
        for wi in range(3):
            s_v.inc(VE.reciprocal(col(28 + wi), col(7 + wi)))
        vsync()
        s_v.inc(VE.tensor_tensor(col(20), col(0), col(7), ALU.mult))
        s_v.inc(VE.tensor_tensor(col(21), col(0), col(8), ALU.mult))
        s_v.inc(VE.tensor_tensor(col(22), col(1), col(9), ALU.mult))
        vsync()
        s_v.inc(VE.reciprocal(col(13), col(20)))
        s_v.inc(VE.reciprocal(col(14), col(21)))
        s_v.inc(VE.reciprocal(col(15), col(22)))
        vsync()
        dsts = [wsg_sl, wsu_sl, wsd_sl]
        for wi, (src_d, pat) in enumerate(srcs):
            s_v.wt(SY)
            s_s.wt(SY)
            if pat:
                s_ld.d(SY.dma_start(wq[:].rearrange("p (j d) -> p j d", d=D),
                                    src_d[:, :].rearrange("(j p) d -> p j d", p=128)))
            else:
                s_ld.d(SY.dma_start(wq[:], src_d[:, :]))
            s_ld.wt(VE)
            wb = wq[:].rearrange("p (b k) -> p b k", k=16)
            s_v.inc(VE.tensor_reduce(qb["amax"][:, 0:256], wb, AXX, ALU.max,
                                     apply_absolute_value=True))
            scale_chain(qb, 256, 7 + wi, 28 + wi)
            for tt in range(4):
                if pat:
                    dap = dsts[wi][tt * 128:(tt + 1) * 128, :]
                else:
                    dap = dsts[wi][:, tt * D:(tt + 1) * D]
                quant_values(qb, wq[:, tt * D:(tt + 1) * D], tt * 64, D, dap, s_gd)
        s_gd.wt(GE)
        s_g.inc(GE.collective_compute("AllGather", ALU.bypass, rg,
                                      [wsg_sl[:, :]], [wsg_f[:, :]]))
        s_g.inc(GE.collective_compute("AllGather", ALU.bypass, rg,
                                      [wsu_sl[:, :]], [wsu_f[:, :]]))
        s_g.inc(GE.collective_compute("AllGather", ALU.bypass, rg,
                                      [wsd_sl[:, :]], [wsd_f[:, :]]))
        barrier()
    g_ws = s_g.n

    pb = [st.enter_context(nc.psum_tensor(f"pb{i}", [128, 512], F32)) for i in range(6)]
    ev = [(None, 0)] * 6

    with ExitStack() as sk:
        xn2T = sbuf("xn2T", [128, 8 * GT], BF16, sk)
        xssm = sbuf("xssm", [128, 12 * GT], F32, sk)
        tanhg = sbuf("tanhg", [128, 12 * GT], F32, sk)
        ybuf = sbuf("ybuf", [128, 12 * GT], F32, sk)
        gatedb = sbuf("gatedb", [128, 12 * GT], BF16, sk)
        htmp = sbuf("htmp", [128, 3 * GT], F32, sk)
        attn = sbuf("attn", [128, D], F32, sk)
        z1b = sbuf("z1b", [128, D], F32, sk)

        for gg in range(NG):
            s_t.wt(SE)
            for kj in range(8):
                s_sc.d(SE.dma_start(
                    xn2T[:, kj * GT:(kj + 1) * GT],
                    xn2_dd[gg * GT:(gg + 1) * GT, kj * 128:(kj + 1) * 128],
                    transpose=True))
            sc_x = s_sc.n
            for oc in range(24):
                bank = oc % 2
                TE.wait_ge(s_sc.h, sc_x)
                sm, cnt = ev[bank]
                if sm is not None:
                    TE.wait_ge(sm.h, cnt)
                for kj in range(8):
                    ins = TE.matmul(pb[bank][:],
                                    winT_bf[:, kj * I3 + oc * 128:kj * I3 + (oc + 1) * 128],
                                    xn2T[:, kj * GT:(kj + 1) * GT],
                                    start=(kj == 0), stop=(kj == 7))
                s_t.inc(ins)
                s_t.wt(VE)
                dst = (xssm if oc < 12 else tanhg)
                j = oc if oc < 12 else oc - 12
                dd = dst[:, j * GT:(j + 1) * GT]
                s_v.inc(VE.tensor_scalar(dd, pb[bank][:], bias_col[:, oc:oc + 1], 5.0,
                                         ALU.add, ALU.min))
                s_v.inc(VE.tensor_scalar(dd, dd, -5.0, None, ALU.max))
                ev[bank] = (s_v, s_v.n)
            s_v.wt(SE)
            for j in range(12):
                s_s.inc(SE.activation(tanhg[:, j * GT:(j + 1) * GT],
                                      tanhg[:, j * GT:(j + 1) * GT], ACT.Tanh))
            pend = []
            for j in range(12):
                xs_j = xssm[:, j * GT:(j + 1) * GT]
                y_j = ybuf[:, j * GT:(j + 1) * GT]
                for d in range(DS):
                    cc = j * 8 + d
                    k3 = (j * 8 + d) % 3
                    ht = htmp[:, k3 * GT:(k3 + 1) * GT]
                    dk = decay_t[:, cc:cc + 1].broadcast_to([128, GT])
                    s_v.inc(VE.tensor_tensor_scan(ht, dk, xs_j,
                                                  carry_t[:, cc:cc + 1],
                                                  ALU.mult, ALU.add))
                    s_v.inc(VE.tensor_scalar(ht, ht, bpos_t[:, cc:cc + 1],
                                             bneg_t[:, cc:cc + 1], ALU.min, ALU.max))
                    pend.append((cc, k3))
                    if len(pend) > 2:
                        pc, pk = pend.pop(0)
                        s_v.inc(VE.tensor_copy(carry_t[:, pc:pc + 1],
                                               htmp[:, pk * GT + GT - 1:pk * GT + GT]))
                    if d == 0:
                        s_v.inc(VE.tensor_scalar(y_j, ht, wyc_t[:, cc:cc + 1],
                                                 None, ALU.mult))
                    else:
                        s_v.inc(VE.scalar_tensor_tensor(y_j, ht,
                                                        wyc_t[:, cc:cc + 1], y_j,
                                                        ALU.mult, ALU.add))
            vsync()
            for pc, pk in pend:
                s_v.inc(VE.tensor_copy(carry_t[:, pc:pc + 1],
                                       htmp[:, pk * GT + GT - 1:pk * GT + GT]))
            pend = []
            s_s.wt(VE)
            s_t.wt(VE)
            for j in range(12):
                y_j = ybuf[:, j * GT:(j + 1) * GT]
                s_v.inc(VE.tensor_scalar(y_j, y_j, 5.0, -5.0, ALU.min, ALU.max))
                s_v.inc(VE.tensor_tensor(gatedb[:, j * GT:(j + 1) * GT], y_j,
                                         tanhg[:, j * GT:(j + 1) * GT], ALU.mult))
            v_gate = s_v.n
            for tc4 in range(4):
                tglob = gg * 4 + tc4
                for oh in range(2):
                    bank = 2 + oh
                    TE.wait_ge(s_v.h, v_gate)
                    sm, cnt = ev[bank]
                    if sm is not None:
                        TE.wait_ge(sm.h, cnt)
                    for kj in range(12):
                        ins = TE.matmul(
                            pb[bank][:],
                            gatedb[:, kj * GT + tc4 * 128:kj * GT + (tc4 + 1) * 128],
                            woutT_bf[:, kj * D + oh * 512:kj * D + (oh + 1) * 512],
                            start=(kj == 0), stop=(kj == 11))
                    s_t.inc(ins)
                    s_t.wt(VE)
                    s_gd.wt(VE)
                    s_v.inc(VE.tensor_scalar(attn[:, oh * 512:(oh + 1) * 512],
                                             pb[bank][:], 5.0, -5.0, ALU.min, ALU.max))
                    ev[bank] = (s_v, s_v.n)
                ln_stats(attn, 20, 21)
                s_v.wt(SE)
                s_gd.wt(SE)
                s_s.inc(SE.activation(z1b[:], attn[:], ACT.Identity, bias=col(21),
                                      scale=col(20)))
                s_s.wt(VE)
                s_v.inc(VE.tensor_tensor(z1b[:], z1b[:], g2rep[:], ALU.mult))
                s_v.inc(VE.tensor_tensor(z1b[:], z1b[:], b2rep[:], ALU.add))
                s_v.wt(GE)
                s_gd.d(GE.dma_start(s_dd[tglob * 128:(tglob + 1) * 128, :], z1b[:]))
        barrier()
    ska.close()

    with ExitStack() as sk:
        xqT = sbuf("xqT", [128, 8 * T], BF16, sk)
        wgT = sbuf("wgT", [128, 8 * 512], BF16, sk)
        wuT = sbuf("wuT", [128, 8 * 512], BF16, sk)
        hidc = sbuf("hidc", [128, NT * 512], F32, sk)
        silu_g = sbuf("silu_g", [128, 512], F32, sk)
        qb = make_qbufs(sk, "h")

        for kj in range(8):
            s_sc.d(SE.dma_start(xqT[:, kj * T:(kj + 1) * T],
                                xq_dd[:, kj * 128:(kj + 1) * 128], transpose=True))
        for ic in range(8):
            s_t.wt(SE)
            for kj in range(8):
                s_sc.d(SE.dma_start(
                    wgT[:, kj * 512:(kj + 1) * 512],
                    wsg_f[ic * 512:(ic + 1) * 512, kj * 128:(kj + 1) * 128],
                    transpose=True))
                s_sc.d(SE.dma_start(
                    wuT[:, kj * 512:(kj + 1) * 512],
                    wsu_f[ic * 512:(ic + 1) * 512, kj * 128:(kj + 1) * 128],
                    transpose=True))
            sc_w = s_sc.n
            for tc in range(NT):
                bg = tc % 2
                bu = 2 + tc % 2
                TE.wait_ge(s_sc.h, sc_w)
                sm, cnt = ev[bg]
                if sm is not None:
                    TE.wait_ge(sm.h, cnt)
                for kj in range(8):
                    ins = TE.matmul(pb[bg][:],
                                    xqT[:, kj * T + tc * 128:kj * T + (tc + 1) * 128],
                                    wgT[:, kj * 512:(kj + 1) * 512],
                                    start=(kj == 0), stop=(kj == 7))
                s_t.inc(ins)
                t_gate = s_t.n
                sm, cnt = ev[bu]
                if sm is not None:
                    TE.wait_ge(sm.h, cnt)
                for kj in range(8):
                    ins = TE.matmul(pb[bu][:],
                                    xqT[:, kj * T + tc * 128:kj * T + (tc + 1) * 128],
                                    wuT[:, kj * 512:(kj + 1) * 512],
                                    start=(kj == 0), stop=(kj == 7))
                s_t.inc(ins)
                t_up = s_t.n
                SE.wait_ge(s_t.h, t_gate)
                s_v.wt(SE)
                s_s.inc(SE.activation(silu_g[:], pb[bg][:], ACT.Silu, scale=col(13)))
                ev[bg] = (s_s, s_s.n)
                VE.wait_ge(s_t.h, t_up)
                s_s.wt(VE)
                s_v.inc(VE.scalar_tensor_tensor(hidc[:, tc * 512:(tc + 1) * 512],
                                                pb[bu][:], col(14), silu_g[:],
                                                ALU.mult, ALU.mult))
                ev[bu] = (s_v, s_v.n)
            for tc in range(NT):
                hb = hidc[:, tc * 512:(tc + 1) * 512].rearrange("p (b k) -> p b k", k=16)
                s_v.inc(VE.tensor_reduce(qb["amax"][:, tc * 32:(tc + 1) * 32], hb, AXX,
                                         ALU.max, apply_absolute_value=True))
            scale_chain(qb, NT * 32, 1, 27)
            for tc in range(NT):
                quant_values(qb, hidc[:, tc * 512:(tc + 1) * 512], tc * 32, 512,
                             hq_dd[tc * 128:(tc + 1) * 128, ic * 512:(ic + 1) * 512],
                             s_gd)
        barrier()

    with ExitStack() as sk:
        hqT = [sbuf("hqT0", [128, 8 * T], BF16, sk), sbuf("hqT1", [128, 8 * T], BF16, sk)]
        wdT = sbuf("wdT", [128, 16 * D], BF16, sk)
        dacc = sbuf("dacc", [128, NT * D], F32, sk)
        gch = sbuf("gch", [128, D], F32, sk)
        xch = sbuf("xch", [128, D], F32, sk)
        sch = sbuf("sch", [128, D], F32, sk)

        for kh in range(2):
            if kh == 1:
                s_t.wt(SE)
            for kj in range(16):
                dstt = hqT[kj // 8]
                s_sc.d(SE.dma_start(
                    dstt[:, (kj % 8) * T:((kj % 8) + 1) * T],
                    hq_dd[:, kh * 2048 + kj * 128:kh * 2048 + (kj + 1) * 128],
                    transpose=True))
            for kj in range(16):
                s_sc.d(SE.dma_start(
                    wdT[:, kj * D:(kj + 1) * D],
                    wsd_f[:, kh * 2048 + kj * 128:kh * 2048 + (kj + 1) * 128],
                    transpose=True))
            sc_h = s_sc.n
            for tc in range(NT):
                for oh in range(2):
                    bank = 4 + oh
                    TE.wait_ge(s_sc.h, sc_h)
                    sm, cnt = ev[bank]
                    if sm is not None:
                        TE.wait_ge(sm.h, cnt)
                    for kj in range(16):
                        src = hqT[kj // 8]
                        ins = TE.matmul(
                            pb[bank][:],
                            src[:, (kj % 8) * T + tc * 128:(kj % 8) * T + (tc + 1) * 128],
                            wdT[:, kj * D + oh * 512:kj * D + (oh + 1) * 512],
                            start=(kj == 0), stop=(kj == 15))
                    s_t.inc(ins)
                    s_t.wt(VE)
                    dst = dacc[:, tc * D + oh * 512:tc * D + (oh + 1) * 512]
                    if kh == 0:
                        s_v.inc(VE.tensor_copy(dst, pb[bank][:]))
                    else:
                        s_v.inc(VE.tensor_tensor(dst, dst, pb[bank][:], ALU.add))
                    ev[bank] = (s_v, s_v.n)
        v_down = s_v.n
        for tc in range(NT):
            s_v.wt(SY)
            s_ld.d(SY.dma_start(xch[:], x_d[tc * 128:(tc + 1) * 128, :]))
            s_ld.d(SY.dma_start(sch[:], s_dd[tc * 128:(tc + 1) * 128, :]))
            ldn = s_ld.n
            SE.wait_ge(s_v.h, v_down)
            s_v.wt(SE)
            s_st.wt(SE)
            s_s.inc(SE.activation(gch[:], dacc[:, tc * D:(tc + 1) * D], ACT.Silu,
                                  scale=col(15)))
            s_s.wt(VE)
            VE.wait_ge(s_ld.h, ldn)
            s_st.wt(VE)
            s_v.inc(VE.tensor_tensor(gch[:], gch[:], sch[:], ALU.mult))
            s_v.inc(VE.tensor_tensor(gch[:], gch[:], xch[:], ALU.add))
            s_v.wt(GE)
            s_st.d(GE.dma_start(out_d[tc * 128:(tc + 1) * 128, :], gch[:]))
        s_st.wt(GE)

    return nc


_NC = None


def kernel(L0_attn_x, L0_attn_W_in, L0_attn_W_out, L0_attn_A_log, L0_attn_B,
           L0_attn_C, L0_attn_ln_in_g, L0_attn_ln_in_b, L0_attn_ln_out_g,
           L0_attn_ln_out_b, L0_ffn_gate_w, L0_ffn_up_w, L0_ffn_down_w,
           L0_ffn_s_in, L0_ffn_s_in_down):
    global _NC
    x = np.ascontiguousarray(np.asarray(L0_attn_x, dtype=np.float32))
    winT = np.ascontiguousarray(np.asarray(L0_attn_W_in, dtype=np.float32).T)
    woutT = np.ascontiguousarray(np.asarray(L0_attn_W_out, dtype=np.float32).T)
    gw = np.ascontiguousarray(np.asarray(L0_ffn_gate_w, dtype=np.float32))
    uw = np.ascontiguousarray(np.asarray(L0_ffn_up_w, dtype=np.float32))
    dw = np.ascontiguousarray(np.asarray(L0_ffn_down_w, dtype=np.float32))
    alog = np.ascontiguousarray(np.asarray(L0_attn_A_log, dtype=np.float32))
    bmat = np.ascontiguousarray(np.asarray(L0_attn_B, dtype=np.float32))
    cmat = np.ascontiguousarray(np.asarray(L0_attn_C, dtype=np.float32))
    lng = np.ascontiguousarray(np.asarray(L0_attn_ln_in_g, dtype=np.float32).reshape(1, D))
    lnb = np.ascontiguousarray(np.asarray(L0_attn_ln_in_b, dtype=np.float32).reshape(1, D))
    lng2 = np.ascontiguousarray(np.asarray(L0_attn_ln_out_g, dtype=np.float32).reshape(1, D))
    lnb2 = np.ascontiguousarray(np.asarray(L0_attn_ln_out_b, dtype=np.float32).reshape(1, D))
    sin = np.ascontiguousarray(np.asarray(L0_ffn_s_in, dtype=np.float32).reshape(1, 1))
    sind = np.ascontiguousarray(np.asarray(L0_ffn_s_in_down, dtype=np.float32).reshape(1, 1))

    if _NC is None:
        _NC = build_nc()
    in_maps = []
    for c in range(NCORE):
        in_maps.append({
            "x": np.ascontiguousarray(x[c]),
            "winT": winT, "woutT": woutT,
            "gate_slice": np.ascontiguousarray(gw[c * 512:(c + 1) * 512]),
            "up_slice": np.ascontiguousarray(uw[c * 512:(c + 1) * 512]),
            "down_slice": np.ascontiguousarray(dw[c * 128:(c + 1) * 128]),
            "a_log": alog, "b_mat": bmat, "c_mat": cmat,
            "ln_in_g": lng, "ln_in_b": lnb, "ln_out_g": lng2, "ln_out_b": lnb2,
            "s_in": sin, "s_in_down": sind,
        })
    res = run_bass_kernel_spmd(_NC, in_maps, core_ids=list(range(NCORE)))
    return np.stack([np.asarray(r["out"], dtype=np.float32) for r in res.results])


# version 13
# speedup vs baseline: 1.2218x; 1.2218x over previous
import numpy as np
from contextlib import ExitStack

import concourse.bass as bass
import concourse.mybir as mybir
from concourse.bass_utils import run_bass_kernel_spmd
from concourse.bass_isa_rust import ReduceOp

F32 = mybir.dt.float32
BF16 = mybir.dt.bfloat16
I32 = mybir.dt.int32
ALU = mybir.AluOpType
ACT = mybir.ActivationFunctionType
AXX = mybir.AxisListType.X

T = 2048
D = 1024
DI = 1536
DS = 8
I3 = 3072
II = 4096
NT = T // 128
NG = 4
GT = 512
NCORE = 8
LN2C = float(np.log(2.0))
LN6C = float(np.log(6.0))


def build_nc():
    nc = bass.Bass("TRN2", target_bir_lowering=False)

    x_d = nc.dram_tensor("x", [T, D], F32, kind="ExternalInput")
    winT_d = nc.dram_tensor("winT", [D, I3], F32, kind="ExternalInput")
    woutT_d = nc.dram_tensor("woutT", [DI, D], F32, kind="ExternalInput")
    gslice_d = nc.dram_tensor("gate_slice", [II // 8, D], F32, kind="ExternalInput")
    uslice_d = nc.dram_tensor("up_slice", [II // 8, D], F32, kind="ExternalInput")
    dslice_d = nc.dram_tensor("down_slice", [D // 8, II], F32, kind="ExternalInput")
    alog_d = nc.dram_tensor("a_log", [DI, DS], F32, kind="ExternalInput")
    bmat_d = nc.dram_tensor("b_mat", [DI, DS], F32, kind="ExternalInput")
    cmat_d = nc.dram_tensor("c_mat", [DI, DS], F32, kind="ExternalInput")
    lng_d = nc.dram_tensor("ln_in_g", [1, D], F32, kind="ExternalInput")
    lnb_d = nc.dram_tensor("ln_in_b", [1, D], F32, kind="ExternalInput")
    lng2_d = nc.dram_tensor("ln_out_g", [1, D], F32, kind="ExternalInput")
    lnb2_d = nc.dram_tensor("ln_out_b", [1, D], F32, kind="ExternalInput")
    sin_d = nc.dram_tensor("s_in", [1, 1], F32, kind="ExternalInput")
    sind_d = nc.dram_tensor("s_in_down", [1, 1], F32, kind="ExternalInput")
    out_d = nc.dram_tensor("out", [T, D], F32, kind="ExternalOutput")

    xq_dd = nc.dram_tensor("xq_dd", [T, D], BF16, kind="Internal")
    xn2_dd = nc.dram_tensor("xn2_dd", [T, D], BF16, kind="Internal")
    s_dd = nc.dram_tensor("s_dd", [T, D], F32, kind="Internal")
    hq_dd = nc.dram_tensor("hq_dd", [T, II], BF16, kind="Internal")
    bias_dd = nc.dram_tensor("bias_dd", [1, I3], F32, kind="Internal")
    wsg_sl = nc.dram_tensor("wsg_sl", [II // 8, D], BF16, kind="Internal")
    wsu_sl = nc.dram_tensor("wsu_sl", [II // 8, D], BF16, kind="Internal")
    wsd_sl = nc.dram_tensor("wsd_sl", [D // 8, II], BF16, kind="Internal")
    wsg_f = nc.dram_tensor("wsg_f", [II, D], BF16, kind="Internal")
    wsu_f = nc.dram_tensor("wsu_f", [II, D], BF16, kind="Internal")
    wsd_f = nc.dram_tensor("wsd_f", [D, II], BF16, kind="Internal")
    am8_in = nc.dram_tensor("am8_in", [1, 8], F32, kind="Internal")
    am8_out = nc.dram_tensor("am8_out", [1, 8], F32, kind="Internal")
    gmx_dd = nc.dram_tensor("gmx_dd", [128, 4], F32, kind="Internal")

    rg = [list(range(NCORE))]
    VE, SE, TE, GE, SY = nc.vector, nc.scalar, nc.tensor, nc.gpsimd, nc.sync

    st = ExitStack()
    nc._keep = st

    def sbuf(name, shape, dt=F32, stk=None):
        return (stk or st).enter_context(nc.sbuf_tensor(name, shape, dt))

    class Sem:
        def __init__(self, name):
            self.h = st.enter_context(nc.semaphore(name))
            self.n = 0
            self.lastw = {}

        def inc(self, instr, k=1):
            self.n += k
            instr.then_inc(self.h, k)

        def d(self, instr):
            self.inc(instr, 16)

        def wt(self, eng, n=None):
            tgt = self.n if n is None else n
            if tgt <= self.lastw.get(id(eng), -1):
                return
            self.lastw[id(eng)] = tgt
            eng.wait_ge(self.h, tgt)

    s_ld = Sem("s_ld")
    s_sc = Sem("s_sc")
    s_gd = Sem("s_gd")
    s_st = Sem("s_st")
    s_v = Sem("s_v")
    s_s = Sem("s_s")
    s_t = Sem("s_t")
    s_g = Sem("s_g")
    ALLSEMS = [s_ld, s_sc, s_gd, s_st, s_v, s_s, s_t, s_g]

    def vsync():
        s_v.wt(VE)

    def barrier():
        for eng in (VE, SE, TE, GE, SY):
            for sm in ALLSEMS:
                if sm.n:
                    sm.wt(eng)

    C = sbuf("smalls", [128, 64])
    decay_t = sbuf("decay_t", [128, 96])
    bpos_t = sbuf("bpos_t", [128, 96])
    bneg_t = sbuf("bneg_t", [128, 96])
    wyc_t = sbuf("wyc_t", [128, 96])
    carry_t = sbuf("carry_t", [128, 96])
    g2rep = sbuf("g2rep", [128, D])
    b2rep = sbuf("b2rep", [128, D])
    bias_col = sbuf("bias_col", [128, 24])
    gcol_t = sbuf("gcol_t", [128, 8])
    bcol_t = sbuf("bcol_t", [128, 8])
    mv6 = sbuf("mv6", [128, 12])
    mv2 = sbuf("mv2", [128, 2])

    def col(i):
        return C[:, i:i + 1]

    _cc = {}

    def ccol(val):
        val = float(val)
        if val not in _cc:
            idx = 32 + len(_cc)
            _cc[val] = idx
            s_v.inc(VE.memset(col(idx), val))
        return col(_cc[val])

    for _v in (1.0, -1.0, -LN6C, 3.0 * LN2C, -LN2C, -4.0 * LN2C, LN2C,
               -1.9999999, -4.4999995):
        ccol(_v)

    def ln_stats(src, cr, cmr):
        s_v.inc(VE.bn_stats(mv6[:, 0:6], src[:, 0:512]))
        s_v.inc(VE.bn_stats(mv6[:, 6:12], src[:, 512:1024]))
        vsync()
        s_v.inc(VE.bn_aggr(mv2[:], mv6[:]))
        vsync()
        s_v.inc(VE.tensor_scalar(col(16), mv2[:, 1:2], 1e-5, None, ALU.add))
        s_v.wt(SE)
        s_s.inc(SE.activation(col(17), col(16), ACT.Sqrt))
        s_s.wt(VE)
        s_v.inc(VE.reciprocal(col(18), col(17)))
        vsync()
        s_v.inc(VE.tensor_tensor(col(19), col(18), col(18), ALU.mult))
        vsync()
        s_v.inc(VE.tensor_tensor(col(19), col(16), col(19), ALU.mult))
        vsync()
        s_v.inc(VE.tensor_scalar(col(19), col(19), -0.5, 1.5, ALU.mult, ALU.add))
        vsync()
        s_v.inc(VE.tensor_tensor(col(cr), col(18), col(19), ALU.mult))
        vsync()
        s_v.inc(VE.tensor_tensor(col(18), mv2[:, 0:1], col(cr), ALU.mult))
        vsync()
        s_v.inc(VE.tensor_scalar(col(cmr), col(18), -1.0, None, ALU.mult))
        vsync()

    def make_qbufs(stk, tag):
        return dict(
            amax=sbuf(f"amax{tag}", [128, 1024], F32, stk),
            sc8h=sbuf(f"sc8h{tag}", [128, 1024], BF16, stk),
            grep=sbuf(f"grep{tag}", [128, 1024], F32, stk),
            t1=sbuf(f"qt1{tag}", [128, 1024], F32, stk),
            t2=sbuf(f"qt2{tag}", [128, 1024], F32, stk),
            ti=sbuf(f"qti{tag}", [128, 1024], I32, stk),
            z=sbuf(f"qz{tag}", [128, 2048], F32, stk),
            az=sbuf(f"qaz{tag}", [128, 2048], F32, stk),
            L=sbuf(f"qL{tag}", [128, 1024], I32, stk),
            G=sbuf(f"qG{tag}", [128, 1024], I32, stk),
            F=sbuf(f"qF{tag}", [128, 1024], I32, stk),
            U=sbuf(f"qU{tag}", [128, 1024], F32, stk),
            U2=sbuf(f"qU2{tag}", [128, 1024], BF16, stk),
            FR=sbuf(f"qFR{tag}", [128, 1024], F32, stk),
            Q=sbuf(f"qQ{tag}", [128, 1024], BF16, stk),
            S=sbuf(f"qS{tag}", [128, 2048], BF16, stk),
            X=sbuf(f"qX{tag}", [128, 2048], BF16, stk),
            _stx=[0, 0],
            _vgf=[0],
        )

    def scale_chain(qb, Fb, gs_col):
        amax, sc8h, grep = qb["amax"], qb["sc8h"], qb["grep"]
        t1, t2, ti = qb["t1"], qb["t2"], qb["ti"]
        s_v.inc(VE.tensor_scalar(col(19), C[:, gs_col:gs_col + 1], 1.0 / 6.0, None,
                                 ALU.mult))
        vsync()
        s_v.inc(VE.tensor_scalar(t1[:, 0:Fb], amax[:, 0:Fb], col(19), 448.0,
                                 ALU.mult, ALU.min))
        s_v.wt(SE)
        s_s.inc(SE.activation(t2[:, 0:Fb], t1[:, 0:Fb], ACT.Ln))
        s_s.wt(VE)
        s_v.inc(VE.tensor_scalar(t2[:, 0:Fb], t2[:, 0:Fb], 1.0 / LN2C, -0.5,
                                 ALU.mult, ALU.add))
        s_v.inc(VE.tensor_scalar(ti[:, 0:Fb], t2[:, 0:Fb], -6.0, None, ALU.max))
        e8i = qb["Q"]
        e8b = qb["S"]
        s_v.wt(SE)
        s_s.inc(SE.activation(e8i[:, 0:Fb], ti[:, 0:Fb], ACT.Exp,
                              bias=ccol(3.0 * LN2C), scale=ccol(-LN2C)))
        s_s.inc(SE.activation(e8b[:, 0:Fb], ti[:, 0:Fb], ACT.Exp,
                              bias=ccol(-4.0 * LN2C), scale=ccol(LN2C)))
        s_s.wt(VE)
        s_v.inc(VE.tensor_tensor(t1[:, 0:Fb], t1[:, 0:Fb], e8i[:, 0:Fb], ALU.mult))
        s_v.inc(VE.tensor_scalar(ti[:, 0:Fb], t1[:, 0:Fb], 1.0, None, ALU.mult))
        s_v.inc(VE.tensor_tensor(sc8h[:, 0:Fb], ti[:, 0:Fb], e8b[:, 0:Fb], ALU.mult))
        s_v.inc(VE.tensor_scalar(t2[:, 0:Fb], sc8h[:, 0:Fb], 2.0, 5e-7,
                                 ALU.mult, ALU.max))
        s_v.inc(VE.reciprocal(t1[:, 0:Fb], t2[:, 0:Fb]))
        s_v.inc(VE.tensor_scalar(grep[:, 0:Fb], t1[:, 0:Fb],
                                 C[:, gs_col:gs_col + 1], None, ALU.mult))

    def quant_s1(qb, src, Fn, par):
        z = qb["z"][:, par * 1024:par * 1024 + Fn]
        az = qb["az"][:, par * 1024:par * 1024 + Fn]
        Sg = qb["S"][:, par * 1024:par * 1024 + Fn]
        nb = Fn // 16
        blk_off = qb["_s1off"]
        grepap = qb["grep"][:, blk_off:blk_off + nb].rearrange(
            "p (b o) -> p b o", o=1).broadcast_to([128, nb, 16])
        s_v.inc(VE.tensor_tensor(z, src, grepap, ALU.mult))
        s_v.wt(SE)
        s_s.inc(SE.activation(az, z, ACT.Abs))
        s_s.inc(SE.activation(Sg, z, ACT.Sign))
        return s_s.n

    def quant_s2(qb, blk_off, Fn, par, dst_ap, store_sem, s_cnt):
        nb = Fn // 16
        sc8ap = qb["sc8h"][:, blk_off:blk_off + nb].rearrange(
            "p (b o) -> p b o", o=1).broadcast_to([128, nb, 16])
        az = qb["az"][:, par * 1024:par * 1024 + Fn]
        Sg = qb["S"][:, par * 1024:par * 1024 + Fn]
        Xs = qb["X"][:, par * 1024:par * 1024 + Fn]
        L, G, F = qb["L"][:, 0:Fn], qb["G"][:, 0:Fn], qb["F"][:, 0:Fn]
        GR, FR = qb["U"][:, 0:Fn], qb["FR"][:, 0:Fn]
        U2, Q = qb["U2"][:, 0:Fn], qb["Q"][:, 0:Fn]
        t1 = qb["t1"][:, 0:Fn]
        t2 = qb["t2"][:, 0:Fn]
        s_v.wt(SE, qb["_vgf"][0])
        s_s.inc(SE.activation(GR, az, ACT.Relu, bias=ccol(-1.9999999)))
        s_s.inc(SE.activation(FR, az, ACT.Relu, bias=ccol(-4.4999995)))
        s_gf = s_s.n
        s_s.wt(VE, s_cnt)
        s_v.inc(VE.tensor_scalar(L, az, 2.0, 4.0, ALU.mult, ALU.min))
        s_v.inc(VE.scalar_tensor_tensor(t2, az, 0.25, L, ALU.is_equal, ALU.add))
        s_v.inc(VE.scalar_tensor_tensor(t1, az, 1.25, t2, ALU.is_equal, ALU.add))
        s_s.wt(VE, s_gf)
        s_v.inc(VE.tensor_scalar(G, GR, 2.0, None, ALU.min))
        s_v.inc(VE.tensor_scalar(F, FR, 1.0, None, ALU.min))
        qb["_vgf"][0] = s_v.n
        s_v.inc(VE.scalar_tensor_tensor(U2, G, 2.0, t1, ALU.mult, ALU.add))
        s_v.inc(VE.scalar_tensor_tensor(Q, F, 4.0, U2, ALU.mult, ALU.add))
        s_v.inc(VE.tensor_tensor(U2, Q, sc8ap, ALU.mult))
        store_sem.wt(VE, qb["_stx"][par])
        s_v.inc(VE.tensor_tensor(Xs, U2, Sg, ALU.mult))
        s_v.wt(GE)
        store_sem.d(GE.dma_start(dst_ap, Xs))
        qb["_stx"][par] = store_sem.n

    def quant_tiles(qb, tiles, store_sem):
        pend = None
        for i, (src, blk_off, Fn, dst) in enumerate(tiles):
            qb["_s1off"] = blk_off
            cnt = quant_s1(qb, src, Fn, i % 2)
            if pend is not None:
                quant_s2(qb, *pend)
            pend = (blk_off, Fn, i % 2, dst, store_sem, cnt)
        if pend is not None:
            quant_s2(qb, *pend)

    s_ld.d(SY.dma_start(decay_t[:].rearrange("p (j d) -> p j d", d=8), alog_d[:, :].rearrange("(j p) d -> p j d", p=128)))
    s_ld.d(SY.dma_start(bpos_t[:].rearrange("p (j d) -> p j d", d=8), bmat_d[:, :].rearrange("(j p) d -> p j d", p=128)))
    s_ld.d(SY.dma_start(wyc_t[:].rearrange("p (j d) -> p j d", d=8), cmat_d[:, :].rearrange("(j p) d -> p j d", p=128)))
    with nc.allow_non_contiguous_dma(reason="tiny per-k-tile gain/bias column loads"):
        s_ld.d(SY.dma_start(gcol_t[:], lng_d[0:1, :].rearrange("o (j p) -> p o j", p=128).squeeze(1)))
        s_ld.d(SY.dma_start(bcol_t[:], lnb_d[0:1, :].rearrange("o (j p) -> p o j", p=128).squeeze(1)))
    with nc.allow_non_contiguous_dma(reason="scalar broadcasts"):
        s_ld.d(SY.dma_start(C[:, 0:1], sin_d[0:1, 0:1].broadcast_to([128, 1])))
        s_ld.d(SY.dma_start(C[:, 1:2], sind_d[0:1, 0:1].broadcast_to([128, 1])))
    s_ld.d(SY.dma_start(g2rep[:], lng2_d[0:1, :].broadcast_to([128, D])))
    s_ld.d(SY.dma_start(b2rep[:], lnb2_d[0:1, :].broadcast_to([128, D])))
    ld0 = s_ld.n
    s_ld.wt(SE, ld0)
    s_s.inc(SE.activation(col(2), col(0), ACT.Ln))
    s_s.inc(SE.activation(col(3), col(1), ACT.Ln))

    s_ld.wt(VE, ld0)
    s_v.inc(VE.tensor_scalar(carry_t[:], decay_t[:], 0.0, -5.0, ALU.min, ALU.max))
    s_v.wt(SE)
    s_s.inc(SE.activation(carry_t[:], carry_t[:], ACT.Exp))
    s_s.wt(VE)
    s_v.inc(VE.tensor_scalar(carry_t[:], carry_t[:], 0.01, 2.0, ALU.max, ALU.min))
    vsync()
    s_v.inc(VE.tensor_scalar(decay_t[:], carry_t[:], -0.9, None, ALU.mult))
    s_v.inc(VE.tensor_tensor(carry_t[:], bpos_t[:], wyc_t[:], ALU.mult))
    vsync()
    s_v.inc(VE.tensor_scalar(wyc_t[:], carry_t[:], 0.1, None, ALU.mult))
    s_v.wt(SE)
    s_s.inc(SE.activation(carry_t[:], bpos_t[:], ACT.Abs))
    s_s.wt(VE)
    s_v.inc(VE.tensor_scalar(carry_t[:], carry_t[:], 1e-30, None, ALU.max))
    vsync()
    s_v.inc(VE.reciprocal(bpos_t[:], carry_t[:]))
    vsync()
    s_v.inc(VE.tensor_scalar(bpos_t[:], bpos_t[:], 50.0, None, ALU.mult))
    vsync()
    s_v.inc(VE.tensor_scalar(bneg_t[:], bpos_t[:], -1.0, None, ALU.mult))
    s_v.inc(VE.memset(carry_t[:], 0.0))

    with ExitStack() as sk:
        sxn = sbuf("sxn", [128, NT * D], F32, sk)
        qb = make_qbufs(sk, "a")
        s_ld.d(SY.dma_start(sxn[:].rearrange("p (n d) -> p n d", d=D), x_d[:, :].rearrange("(n p) d -> p n d", p=128)))
        ldx = s_ld.n
        s_ld.wt(VE, ldx)
        s_ld.wt(SE, ldx)
        for n in range(NT):
            xt = sxn[:, n * D:(n + 1) * D]
            ln_stats(xt, 20, 21)
            s_v.wt(SE)
            s_s.inc(SE.activation(xt, xt, ACT.Identity, bias=col(21), scale=col(20)))
            s_s.wt(VE)
        for n in range(NT):
            xt = sxn[:, n * D:(n + 1) * D]
            ln_stats(xt, 20, 21)
            s_v.wt(SE)
            s_gd.wt(SE)
            s_s.inc(SE.activation(qb["X"][:, 0:D], xt, ACT.Identity, bias=col(21),
                                  scale=col(20)))
            s_s.wt(GE)
            s_gd.d(GE.dma_start(xn2_dd[n * 128:(n + 1) * 128, :], qb["X"][:, 0:D]))
            xb = xt.rearrange("p (b k) -> p b k", k=16)
            s_v.inc(VE.tensor_reduce(qb["amax"][:, n * 64:(n + 1) * 64], xb, AXX,
                                     ALU.max, apply_absolute_value=True))
        scale_chain(qb, NT * 64, 0)
        qb["_stx"] = [s_gd.n, s_gd.n]
        quant_tiles(qb, [(sxn[:, n * D:(n + 1) * D], n * 64, D,
                          xq_dd[n * 128:(n + 1) * 128, :]) for n in range(NT)],
                    s_gd)

    ska = ExitStack()
    winT_bf = sbuf("winT_bf", [128, 8 * I3], BF16, ska)
    woutT_bf = sbuf("woutT_bf", [128, 12 * D], BF16, ska)
    with ExitStack() as sk:
        wtmp = sbuf("wtmp", [128, 2 * I3], F32, sk)
        row1 = sbuf("row1", [1, I3], F32, sk)
        pbias = [sk.enter_context(nc.psum_tensor(f"pbias{i}", [1, 512], F32)) for i in range(6)]
        s_v.wt(SY)
        s_s.wt(SY)
        s_gd.wt(SY)
        s_gd.d(GE.dma_start(woutT_bf[:].rearrange("p (j d) -> p j d", d=D),
                            woutT_d[:, :].rearrange("(j p) d -> p j d", p=128)))
        for kj in range(8):
            buf = wtmp[:, (kj % 2) * I3:(kj % 2) * I3 + I3]
            if kj >= 2:
                s_s.wt(SY)
                s_t.wt(SY)
            s_ld.d(SY.dma_start(buf, winT_d[kj * 128:(kj + 1) * 128, :]))
            s_ld.wt(SE)
            s_s.inc(SE.activation(winT_bf[:, kj * I3:(kj + 1) * I3], buf, ACT.Copy,
                                  scale=gcol_t[:, kj:kj + 1]))
            s_ld.wt(TE)
            for c in range(6):
                ins = TE.matmul(pbias[c][:], bcol_t[:, kj:kj + 1],
                                buf[:, c * 512:(c + 1) * 512],
                                start=(kj == 0), stop=(kj == 7))
            s_t.inc(ins)
        s_t.wt(VE)
        for c in range(6):
            s_v.inc(VE.tensor_copy(row1[0:1, c * 512:(c + 1) * 512], pbias[c][:]))
        s_v.wt(GE)
        s_gd.d(GE.dma_start(bias_dd[0:1, :], row1[:]))
        s_gd.wt(SY)
        with nc.allow_non_contiguous_dma(reason="tiny bias column reload"):
            s_ld.d(SY.dma_start(bias_col[:, 0:24],
                                bias_dd[0:1, :].rearrange("o (c p) -> p o c", p=128).squeeze(1)))
    ld_bias = s_ld.n

    with ExitStack() as sk:
        wq = sbuf("wq", [128, 4 * D], F32, sk)
        qb = make_qbufs(sk, "w")
        srcs = [(gslice_d, 1), (uslice_d, 1), (dslice_d, None)]
        s_s.wt(SY)
        s_t.wt(SY)
        for wi, (src_d, pat) in enumerate(srcs):
            s_v.wt(SY)
            if pat:
                s_ld.d(SY.dma_start(wq[:].rearrange("p (j d) -> p j d", d=D),
                                    src_d[:, :].rearrange("(j p) d -> p j d", p=128)))
            else:
                s_ld.d(SY.dma_start(wq[:], src_d[:, :]))
            s_ld.wt(VE)
            wb = wq[:].rearrange("p (b k) -> p b k", k=16)
            s_v.inc(VE.tensor_reduce(qb["amax"][:, 0:256], wb, AXX, ALU.max,
                                     apply_absolute_value=True))
            s_v.inc(VE.tensor_reduce(col(4 + wi), qb["amax"][:, 0:256], AXX, ALU.max))
        s_v.wt(GE)
        s_gd.d(GE.dma_start(gmx_dd[:, :], C[:, 4:8]))
        s_gd.wt(SY)
        with nc.allow_non_contiguous_dma(reason="gmax transpose readback"):
            for wi in range(3):
                s_ld.d(SY.dma_start(qb["t1"][0:1, wi * 128:(wi + 1) * 128],
                                    gmx_dd[:, wi:wi + 1].transpose([1, 0])))
        s_ld.wt(VE)
        s_v.inc(VE.memset(qb["t2"][0:1, 0:8], 0.0))
        for wi in range(3):
            s_v.inc(VE.tensor_reduce(qb["t2"][0:1, wi:wi + 1],
                                     qb["t1"][0:1, wi * 128:(wi + 1) * 128],
                                     AXX, ALU.max))
        s_v.wt(GE)
        s_gd.d(GE.dma_start(am8_in[0:1, :], qb["t2"][0:1, 0:8]))
        s_gd.wt(GE)
        s_g.inc(GE.collective_compute("AllReduce", ALU.max, rg,
                                      [am8_in[:, :]], [am8_out[:, :]]))
        s_g.wt(SY)
        with nc.allow_non_contiguous_dma(reason="gmax broadcast"):
            for wi in range(3):
                s_ld.d(SY.dma_start(C[:, 4 + wi:5 + wi],
                                    am8_out[0:1, wi:wi + 1].broadcast_to([128, 1])))
        s_ld.wt(VE)
        for wi in range(3):
            s_v.inc(VE.reciprocal(col(20 + wi), col(4 + wi)))
        vsync()
        for wi in range(3):
            s_v.inc(VE.tensor_scalar(col(7 + wi), col(20 + wi), 2688.0, None, ALU.mult))
        vsync()
        s_v.wt(SE)
        for wi in range(3):
            s_s.inc(SE.activation(col(10 + wi), col(7 + wi), ACT.Ln))
        s_v.inc(VE.tensor_tensor(col(20), col(0), col(7), ALU.mult))
        s_v.inc(VE.tensor_tensor(col(21), col(0), col(8), ALU.mult))
        s_v.inc(VE.tensor_tensor(col(22), col(1), col(9), ALU.mult))
        vsync()
        s_v.inc(VE.reciprocal(col(13), col(20)))
        s_v.inc(VE.reciprocal(col(14), col(21)))
        s_v.inc(VE.reciprocal(col(15), col(22)))
        vsync()
        dsts = [wsg_sl, wsu_sl, wsd_sl]
        for wi, (src_d, pat) in enumerate(srcs):
            s_v.wt(SY)
            s_s.wt(SY)
            if pat:
                s_ld.d(SY.dma_start(wq[:].rearrange("p (j d) -> p j d", d=D),
                                    src_d[:, :].rearrange("(j p) d -> p j d", p=128)))
            else:
                s_ld.d(SY.dma_start(wq[:], src_d[:, :]))
            s_ld.wt(VE)
            wb = wq[:].rearrange("p (b k) -> p b k", k=16)
            s_v.inc(VE.tensor_reduce(qb["amax"][:, 0:256], wb, AXX, ALU.max,
                                     apply_absolute_value=True))
            scale_chain(qb, 256, 7 + wi)
            tiles = []
            for tt in range(4):
                if pat:
                    dap = dsts[wi][tt * 128:(tt + 1) * 128, :]
                else:
                    dap = dsts[wi][:, tt * D:(tt + 1) * D]
                tiles.append((wq[:, tt * D:(tt + 1) * D], tt * 64, D, dap))
            quant_tiles(qb, tiles, s_gd)
        s_gd.wt(GE)
        s_g.inc(GE.collective_compute("AllGather", ALU.bypass, rg,
                                      [wsg_sl[:, :]], [wsg_f[:, :]]))
        s_g.inc(GE.collective_compute("AllGather", ALU.bypass, rg,
                                      [wsu_sl[:, :]], [wsu_f[:, :]]))
        s_g.inc(GE.collective_compute("AllGather", ALU.bypass, rg,
                                      [wsd_sl[:, :]], [wsd_f[:, :]]))
    g_ws = s_g.n

    pb = [st.enter_context(nc.psum_tensor(f"pb{i}", [128, 512], F32)) for i in range(6)]
    ev = [(None, 0)] * 6

    with ExitStack() as sk:
        xn2T = sbuf("xn2T", [128, 8 * GT], BF16, sk)
        xssm = sbuf("xssm", [128, 12 * GT], F32, sk)
        tanhg = sbuf("tanhg", [128, 12 * GT], F32, sk)
        ybuf = sbuf("ybuf", [128, 12 * GT], F32, sk)
        gatedb = sbuf("gatedb", [128, 12 * GT], BF16, sk)
        htmp = sbuf("htmp", [128, 3 * GT], F32, sk)
        attn = sbuf("attn", [128, D], F32, sk)
        z1b = sbuf("z1b", [128, D], F32, sk)

        for qe in (SY, SE):
            s_v.wt(qe)
            s_gd.wt(qe)
        s_s.wt(SY)
        for gg in range(NG):
            s_t.wt(SY)
            s_t.wt(SE)
            for kj in range(8):
                s_sc.d(SY.dma_start(
                    xn2T[:, kj * GT:(kj + 1) * GT],
                    xn2_dd[gg * GT:(gg + 1) * GT, kj * 128:(kj + 1) * 128],
                    transpose=True))
            sc_x = s_sc.n
            s_ld.wt(VE, ld_bias)
            for oc in range(24):
                bank = oc % 2
                s_sc.wt(TE, sc_x)
                sm, cnt = ev[bank]
                if sm is not None:
                    sm.wt(TE, cnt)
                for kj in range(8):
                    ins = TE.matmul(pb[bank][:],
                                    winT_bf[:, kj * I3 + oc * 128:kj * I3 + (oc + 1) * 128],
                                    xn2T[:, kj * GT:(kj + 1) * GT],
                                    start=(kj == 0), stop=(kj == 7))
                s_t.inc(ins)
                s_t.wt(VE)
                dst = (xssm if oc < 12 else tanhg)
                j = oc if oc < 12 else oc - 12
                dd = dst[:, j * GT:(j + 1) * GT]
                s_v.inc(VE.tensor_scalar(dd, pb[bank][:], bias_col[:, oc:oc + 1], 5.0,
                                         ALU.add, ALU.min))
                s_v.inc(VE.tensor_scalar(dd, dd, -5.0, None, ALU.max))
                ev[bank] = (s_v, s_v.n)
            s_v.wt(SE)
            for j in range(12):
                s_s.inc(SE.activation(tanhg[:, j * GT:(j + 1) * GT],
                                      tanhg[:, j * GT:(j + 1) * GT], ACT.Tanh))
            pend = []
            for j in range(12):
                xs_j = xssm[:, j * GT:(j + 1) * GT]
                y_j = ybuf[:, j * GT:(j + 1) * GT]
                for d in range(DS):
                    cc = j * 8 + d
                    k3 = (j * 8 + d) % 3
                    ht = htmp[:, k3 * GT:(k3 + 1) * GT]
                    dk = decay_t[:, cc:cc + 1].broadcast_to([128, GT])
                    s_v.inc(VE.tensor_tensor_scan(ht, dk, xs_j,
                                                  carry_t[:, cc:cc + 1],
                                                  ALU.mult, ALU.add))
                    s_v.inc(VE.tensor_scalar(ht, ht, bpos_t[:, cc:cc + 1],
                                             bneg_t[:, cc:cc + 1], ALU.min, ALU.max))
                    pend.append((cc, k3))
                    if len(pend) > 2:
                        pc, pk = pend.pop(0)
                        s_v.inc(VE.tensor_copy(carry_t[:, pc:pc + 1],
                                               htmp[:, pk * GT + GT - 1:pk * GT + GT]))
                    if d == 0:
                        s_v.inc(VE.tensor_scalar(y_j, ht, wyc_t[:, cc:cc + 1],
                                                 None, ALU.mult))
                    else:
                        s_v.inc(VE.scalar_tensor_tensor(y_j, ht,
                                                        wyc_t[:, cc:cc + 1], y_j,
                                                        ALU.mult, ALU.add))
            vsync()
            for pc, pk in pend:
                s_v.inc(VE.tensor_copy(carry_t[:, pc:pc + 1],
                                       htmp[:, pk * GT + GT - 1:pk * GT + GT]))
            pend = []
            s_s.wt(VE)
            s_t.wt(VE)
            for j in range(12):
                y_j = ybuf[:, j * GT:(j + 1) * GT]
                s_v.inc(VE.tensor_scalar(y_j, y_j, 5.0, -5.0, ALU.min, ALU.max))
                s_v.inc(VE.tensor_tensor(gatedb[:, j * GT:(j + 1) * GT], y_j,
                                         tanhg[:, j * GT:(j + 1) * GT], ALU.mult))
            v_gate = s_v.n
            for tc4 in range(4):
                tglob = gg * 4 + tc4
                for oh in range(2):
                    bank = 2 + oh
                    s_v.wt(TE, v_gate)
                    sm, cnt = ev[bank]
                    if sm is not None:
                        sm.wt(TE, cnt)
                    for kj in range(12):
                        ins = TE.matmul(
                            pb[bank][:],
                            gatedb[:, kj * GT + tc4 * 128:kj * GT + (tc4 + 1) * 128],
                            woutT_bf[:, kj * D + oh * 512:kj * D + (oh + 1) * 512],
                            start=(kj == 0), stop=(kj == 11))
                    s_t.inc(ins)
                    s_t.wt(VE)
                    s_gd.wt(VE)
                    s_v.inc(VE.tensor_scalar(attn[:, oh * 512:(oh + 1) * 512],
                                             pb[bank][:], 5.0, -5.0, ALU.min, ALU.max))
                    ev[bank] = (s_v, s_v.n)
                ln_stats(attn, 20, 21)
                s_v.wt(SE)
                s_gd.wt(SE)
                s_s.inc(SE.activation(z1b[:], attn[:], ACT.Identity, bias=col(21),
                                      scale=col(20)))
                s_s.wt(VE)
                s_v.inc(VE.tensor_tensor(z1b[:], z1b[:], g2rep[:], ALU.mult))
                s_v.inc(VE.tensor_tensor(z1b[:], z1b[:], b2rep[:], ALU.add))
                s_v.wt(GE)
                s_gd.d(GE.dma_start(s_dd[tglob * 128:(tglob + 1) * 128, :], z1b[:]))
    ska.close()

    with ExitStack() as sk:
        xqT = sbuf("xqT", [128, 8 * T], BF16, sk)
        wgT = sbuf("wgT", [128, 8 * 512], BF16, sk)
        wuT = sbuf("wuT", [128, 8 * 512], BF16, sk)
        hidc = sbuf("hidc", [128, NT * 512], F32, sk)
        silu_g = sbuf("silu_g", [128, 2 * 512], F32, sk)
        qb = make_qbufs(sk, "h")

        for qe in (SY, SE):
            s_v.wt(qe)
            s_t.wt(qe)
            s_gd.wt(qe)
            s_g.wt(qe, g_ws)
        s_s.wt(SY)
        for kj in range(8):
            s_sc.d(SY.dma_start(
                xqT[:, kj * T:(kj + 1) * T],
                xq_dd[:, kj * 128:(kj + 1) * 128], transpose=True))
        for ic in range(8):
            s_t.wt(SY)
            s_t.wt(SE)
            for kj in range(8):
                s_sc.d(SY.dma_start(
                    wgT[:, kj * 512:(kj + 1) * 512],
                    wsg_f[ic * 512:(ic + 1) * 512, kj * 128:(kj + 1) * 128],
                    transpose=True))
                s_sc.d(SY.dma_start(
                    wuT[:, kj * 512:(kj + 1) * 512],
                    wsu_f[ic * 512:(ic + 1) * 512, kj * 128:(kj + 1) * 128],
                    transpose=True))
            sc_w = s_sc.n
            v_sg = [0, 0]
            for tc in range(NT):
                bg = tc % 2
                bu = 2 + tc % 2
                s_sc.wt(TE, sc_w)
                sm, cnt = ev[bg]
                if sm is not None:
                    sm.wt(TE, cnt)
                for kj in range(8):
                    ins = TE.matmul(pb[bg][:],
                                    xqT[:, kj * T + tc * 128:kj * T + (tc + 1) * 128],
                                    wgT[:, kj * 512:(kj + 1) * 512],
                                    start=(kj == 0), stop=(kj == 7))
                s_t.inc(ins)
                t_gate = s_t.n
                sm, cnt = ev[bu]
                if sm is not None:
                    sm.wt(TE, cnt)
                for kj in range(8):
                    ins = TE.matmul(pb[bu][:],
                                    xqT[:, kj * T + tc * 128:kj * T + (tc + 1) * 128],
                                    wuT[:, kj * 512:(kj + 1) * 512],
                                    start=(kj == 0), stop=(kj == 7))
                s_t.inc(ins)
                t_up = s_t.n
                sg = silu_g[:, (tc % 2) * 512:(tc % 2) * 512 + 512]
                s_t.wt(SE, t_gate)
                s_v.wt(SE, v_sg[tc % 2])
                s_s.inc(SE.activation(sg, pb[bg][:], ACT.Silu, scale=col(13)))
                ev[bg] = (s_s, s_s.n)
                s_t.wt(VE, t_up)
                s_s.wt(VE)
                s_v.inc(VE.scalar_tensor_tensor(hidc[:, tc * 512:(tc + 1) * 512],
                                                pb[bu][:], col(14), sg,
                                                ALU.mult, ALU.mult))
                ev[bu] = (s_v, s_v.n)
                v_sg[tc % 2] = s_v.n
            for tc in range(NT):
                hb = hidc[:, tc * 512:(tc + 1) * 512].rearrange("p (b k) -> p b k", k=16)
                s_v.inc(VE.tensor_reduce(qb["amax"][:, tc * 32:(tc + 1) * 32], hb, AXX,
                                         ALU.max, apply_absolute_value=True))
            scale_chain(qb, NT * 32, 1)
            quant_tiles(qb, [(hidc[:, tc * 512:(tc + 1) * 512], tc * 32, 512,
                              hq_dd[tc * 128:(tc + 1) * 128,
                                    ic * 512:(ic + 1) * 512])
                             for tc in range(NT)], s_gd)
        barrier()

    with ExitStack() as sk:
        TH = T // 2
        hqTh = sbuf("hqTh", [128, 32 * TH], BF16, sk)
        wdT2 = sbuf("wdT2", [128, 32 * D], BF16, sk)
        gch = sbuf("gch", [128, 2 * D], F32, sk)
        xch = sbuf("xch", [128, 2 * D], F32, sk)
        sch = sbuf("sch", [128, 2 * D], F32, sk)

        for qe in (SY, SE):
            s_v.wt(qe)
            s_t.wt(qe)
            s_gd.wt(qe)
        s_s.wt(SY)
        for q in range(32):
            s_sc.d(SY.dma_start(
                wdT2[:, q * D:(q + 1) * D],
                wsd_f[:, q * 128:(q + 1) * 128],
                transpose=True))
        v_xs = [0, 0]
        st_g = [0, 0]
        for th in range(2):
            if th == 1:
                s_t.wt(SY)
                s_t.wt(SE)
            for q in range(32):
                s_sc.d(SY.dma_start(
                    hqTh[:, q * TH:(q + 1) * TH],
                    hq_dd[th * TH:(th + 1) * TH, q * 128:(q + 1) * 128],
                    transpose=True))
            sc_h = s_sc.n
            for tci in range(8):
                tc = th * 8 + tci
                par = tc % 2
                s_v.wt(SY, v_xs[par])
                s_ld.d(SY.dma_start(xch[:, par * D:(par + 1) * D],
                                    x_d[tc * 128:(tc + 1) * 128, :]))
                s_ld.d(SY.dma_start(sch[:, par * D:(par + 1) * D],
                                    s_dd[tc * 128:(tc + 1) * 128, :]))
                ldn = s_ld.n
                t_dn = [0, 0]
                for oh in range(2):
                    bank = par * 2 + oh
                    s_sc.wt(TE, sc_h)
                    sm, cnt = ev[bank]
                    if sm is not None:
                        sm.wt(TE, cnt)
                    for q in range(32):
                        ins = TE.matmul(
                            pb[bank][:],
                            hqTh[:, q * TH + tci * 128:q * TH + (tci + 1) * 128],
                            wdT2[:, q * D + oh * 512:q * D + (oh + 1) * 512],
                            start=(q == 0), stop=(q == 31))
                    s_t.inc(ins)
                    t_dn[oh] = s_t.n
                g = gch[:, par * D:(par + 1) * D]
                s_st.wt(SE, st_g[par])
                for oh in range(2):
                    s_t.wt(SE, t_dn[oh])
                    s_s.inc(SE.activation(g[:, oh * 512:(oh + 1) * 512],
                                          pb[par * 2 + oh][:], ACT.Silu,
                                          scale=col(15)))
                    ev[par * 2 + oh] = (s_s, s_s.n)
                s_s.wt(VE)
                s_ld.wt(VE, ldn)
                s_v.inc(VE.tensor_tensor(g, g, sch[:, par * D:(par + 1) * D],
                                         ALU.mult))
                s_v.inc(VE.tensor_tensor(g, g, xch[:, par * D:(par + 1) * D],
                                         ALU.add))
                v_xs[par] = s_v.n
                s_v.wt(GE)
                s_st.d(GE.dma_start(out_d[tc * 128:(tc + 1) * 128, :], g))
                st_g[par] = s_st.n
        s_st.wt(GE)

    return nc


_NC = None


def kernel(L0_attn_x, L0_attn_W_in, L0_attn_W_out, L0_attn_A_log, L0_attn_B,
           L0_attn_C, L0_attn_ln_in_g, L0_attn_ln_in_b, L0_attn_ln_out_g,
           L0_attn_ln_out_b, L0_ffn_gate_w, L0_ffn_up_w, L0_ffn_down_w,
           L0_ffn_s_in, L0_ffn_s_in_down):
    global _NC
    x = np.ascontiguousarray(np.asarray(L0_attn_x, dtype=np.float32))
    winT = np.ascontiguousarray(np.asarray(L0_attn_W_in, dtype=np.float32).T)
    woutT = np.ascontiguousarray(np.asarray(L0_attn_W_out, dtype=np.float32).T)
    gw = np.ascontiguousarray(np.asarray(L0_ffn_gate_w, dtype=np.float32))
    uw = np.ascontiguousarray(np.asarray(L0_ffn_up_w, dtype=np.float32))
    dw = np.ascontiguousarray(np.asarray(L0_ffn_down_w, dtype=np.float32))
    alog = np.ascontiguousarray(np.asarray(L0_attn_A_log, dtype=np.float32))
    bmat = np.ascontiguousarray(np.asarray(L0_attn_B, dtype=np.float32))
    cmat = np.ascontiguousarray(np.asarray(L0_attn_C, dtype=np.float32))
    lng = np.ascontiguousarray(np.asarray(L0_attn_ln_in_g, dtype=np.float32).reshape(1, D))
    lnb = np.ascontiguousarray(np.asarray(L0_attn_ln_in_b, dtype=np.float32).reshape(1, D))
    lng2 = np.ascontiguousarray(np.asarray(L0_attn_ln_out_g, dtype=np.float32).reshape(1, D))
    lnb2 = np.ascontiguousarray(np.asarray(L0_attn_ln_out_b, dtype=np.float32).reshape(1, D))
    sin = np.ascontiguousarray(np.asarray(L0_ffn_s_in, dtype=np.float32).reshape(1, 1))
    sind = np.ascontiguousarray(np.asarray(L0_ffn_s_in_down, dtype=np.float32).reshape(1, 1))

    if _NC is None:
        _NC = build_nc()
    in_maps = []
    for c in range(NCORE):
        in_maps.append({
            "x": np.ascontiguousarray(x[c]),
            "winT": winT, "woutT": woutT,
            "gate_slice": np.ascontiguousarray(gw[c * 512:(c + 1) * 512]),
            "up_slice": np.ascontiguousarray(uw[c * 512:(c + 1) * 512]),
            "down_slice": np.ascontiguousarray(dw[c * 128:(c + 1) * 128]),
            "a_log": alog, "b_mat": bmat, "c_mat": cmat,
            "ln_in_g": lng, "ln_in_b": lnb, "ln_out_g": lng2, "ln_out_b": lnb2,
            "s_in": sin, "s_in_down": sind,
        })
    res = run_bass_kernel_spmd(_NC, in_maps, core_ids=list(range(NCORE)))
    return np.stack([np.asarray(r["out"], dtype=np.float32) for r in res.results])


# version 15
# speedup vs baseline: 1.3430x; 1.0992x over previous
import numpy as np
from contextlib import ExitStack

import concourse.bass as bass
import concourse.mybir as mybir
from concourse.bass_utils import run_bass_kernel_spmd
from concourse.bass_isa_rust import ReduceOp

F32 = mybir.dt.float32
BF16 = mybir.dt.bfloat16
I32 = mybir.dt.int32
ALU = mybir.AluOpType
ACT = mybir.ActivationFunctionType
AXX = mybir.AxisListType.X

T = 2048
D = 1024
DI = 1536
DS = 8
I3 = 3072
II = 4096
NT = T // 128
NG = 4
GT = 512
NCORE = 8
LN2C = float(np.log(2.0))
LN6C = float(np.log(6.0))


def build_nc():
    nc = bass.Bass("TRN2", target_bir_lowering=False)

    x_d = nc.dram_tensor("x", [T, D], F32, kind="ExternalInput")
    winT_d = nc.dram_tensor("winT", [D, I3], F32, kind="ExternalInput")
    woutT_d = nc.dram_tensor("woutT", [DI, D], F32, kind="ExternalInput")
    gslice_d = nc.dram_tensor("gate_slice", [II // 8, D], F32, kind="ExternalInput")
    uslice_d = nc.dram_tensor("up_slice", [II // 8, D], F32, kind="ExternalInput")
    dslice_d = nc.dram_tensor("down_slice", [D // 8, II], F32, kind="ExternalInput")
    alog_d = nc.dram_tensor("a_log", [DI, DS], F32, kind="ExternalInput")
    bmat_d = nc.dram_tensor("b_mat", [DI, DS], F32, kind="ExternalInput")
    cmat_d = nc.dram_tensor("c_mat", [DI, DS], F32, kind="ExternalInput")
    lng_d = nc.dram_tensor("ln_in_g", [1, D], F32, kind="ExternalInput")
    lnb_d = nc.dram_tensor("ln_in_b", [1, D], F32, kind="ExternalInput")
    lng2_d = nc.dram_tensor("ln_out_g", [1, D], F32, kind="ExternalInput")
    lnb2_d = nc.dram_tensor("ln_out_b", [1, D], F32, kind="ExternalInput")
    sin_d = nc.dram_tensor("s_in", [1, 1], F32, kind="ExternalInput")
    sind_d = nc.dram_tensor("s_in_down", [1, 1], F32, kind="ExternalInput")
    out_d = nc.dram_tensor("out", [T, D], F32, kind="ExternalOutput")

    xq_dd = nc.dram_tensor("xq_dd", [T, D], BF16, kind="Internal")
    xn2_dd = nc.dram_tensor("xn2_dd", [T, D], BF16, kind="Internal")
    s_dd = nc.dram_tensor("s_dd", [T, D], F32, kind="Internal")
    hq_dd = nc.dram_tensor("hq_dd", [T, II], BF16, kind="Internal")
    bias_dd = nc.dram_tensor("bias_dd", [1, I3], F32, kind="Internal")
    wsg_sl = nc.dram_tensor("wsg_sl", [II // 8, D], BF16, kind="Internal")
    wsu_sl = nc.dram_tensor("wsu_sl", [II // 8, D], BF16, kind="Internal")
    wsd_sl = nc.dram_tensor("wsd_sl", [D // 8, II], BF16, kind="Internal")
    wsg_f = nc.dram_tensor("wsg_f", [II, D], BF16, kind="Internal")
    wsu_f = nc.dram_tensor("wsu_f", [II, D], BF16, kind="Internal")
    wsd_f = nc.dram_tensor("wsd_f", [D, II], BF16, kind="Internal")
    am8_in = nc.dram_tensor("am8_in", [1, 8], F32, kind="Internal")
    am8_out = nc.dram_tensor("am8_out", [1, 8], F32, kind="Internal")
    gmx_dd = nc.dram_tensor("gmx_dd", [128, 4], F32, kind="Internal")

    rg = [list(range(NCORE))]
    VE, SE, TE, GE, SY = nc.vector, nc.scalar, nc.tensor, nc.gpsimd, nc.sync

    st = ExitStack()
    nc._keep = st

    def sbuf(name, shape, dt=F32, stk=None):
        return (stk or st).enter_context(nc.sbuf_tensor(name, shape, dt))

    class Sem:
        def __init__(self, name):
            self.h = st.enter_context(nc.semaphore(name))
            self.n = 0
            self.lastw = {}

        def inc(self, instr, k=1):
            self.n += k
            instr.then_inc(self.h, k)

        def d(self, instr):
            self.inc(instr, 16)

        def wt(self, eng, n=None):
            tgt = self.n if n is None else n
            if tgt <= self.lastw.get(id(eng), -1):
                return
            self.lastw[id(eng)] = tgt
            eng.wait_ge(self.h, tgt)

    s_ld = Sem("s_ld")
    s_sc = Sem("s_sc")
    s_gd = Sem("s_gd")
    s_st = Sem("s_st")
    s_v = Sem("s_v")
    s_s = Sem("s_s")
    s_t = Sem("s_t")
    s_g = Sem("s_g")
    ALLSEMS = [s_ld, s_sc, s_gd, s_st, s_v, s_s, s_t, s_g]

    def vsync():
        s_v.wt(VE)

    def barrier():
        for eng in (VE, SE, TE, GE, SY):
            for sm in ALLSEMS:
                if sm.n:
                    sm.wt(eng)

    C = sbuf("smalls", [128, 64])
    decay_t = sbuf("decay_t", [128, 96])
    bpos_t = sbuf("bpos_t", [128, 96])
    bneg_t = sbuf("bneg_t", [128, 96])
    wyc_t = sbuf("wyc_t", [128, 96])
    carry_t = sbuf("carry_t", [128, 96])
    g2rep = sbuf("g2rep", [128, D])
    b2rep = sbuf("b2rep", [128, D])
    bias_col = sbuf("bias_col", [128, 24])
    gcol_t = sbuf("gcol_t", [128, 8])
    bcol_t = sbuf("bcol_t", [128, 8])
    mv6 = sbuf("mv6", [128, 12])
    mv2 = sbuf("mv2", [128, 2])

    def col(i):
        return C[:, i:i + 1]

    _cc = {}

    def ccol(val):
        val = float(val)
        if val not in _cc:
            idx = 32 + len(_cc)
            _cc[val] = idx
            s_v.inc(VE.memset(col(idx), val))
        return col(_cc[val])

    for _v in (1.0, -1.0, -LN6C, 3.0 * LN2C, -LN2C, -4.0 * LN2C, LN2C,
               -1.9999999, -4.4999995):
        ccol(_v)

    def ln_stats(src, cr, cmr):
        s_v.inc(VE.bn_stats(mv6[:, 0:6], src[:, 0:512]))
        s_v.inc(VE.bn_stats(mv6[:, 6:12], src[:, 512:1024]))
        vsync()
        s_v.inc(VE.bn_aggr(mv2[:], mv6[:]))
        vsync()
        s_v.inc(VE.tensor_scalar(col(16), mv2[:, 1:2], 1e-5, None, ALU.add))
        s_v.wt(SE)
        s_s.inc(SE.activation(col(17), col(16), ACT.Sqrt))
        s_s.wt(VE)
        s_v.inc(VE.reciprocal(col(18), col(17)))
        vsync()
        s_v.inc(VE.tensor_tensor(col(19), col(18), col(18), ALU.mult))
        vsync()
        s_v.inc(VE.tensor_tensor(col(19), col(16), col(19), ALU.mult))
        vsync()
        s_v.inc(VE.tensor_scalar(col(19), col(19), -0.5, 1.5, ALU.mult, ALU.add))
        vsync()
        s_v.inc(VE.tensor_tensor(col(cr), col(18), col(19), ALU.mult))
        vsync()
        s_v.inc(VE.tensor_tensor(col(18), mv2[:, 0:1], col(cr), ALU.mult))
        vsync()
        s_v.inc(VE.tensor_scalar(col(cmr), col(18), -1.0, None, ALU.mult))
        vsync()

    def make_qbufs(stk, tag):
        return dict(
            amax=sbuf(f"amax{tag}", [128, 1024], F32, stk),
            sc8h=sbuf(f"sc8h{tag}", [128, 1024], BF16, stk),
            grep=sbuf(f"grep{tag}", [128, 1024], F32, stk),
            t1=sbuf(f"qt1{tag}", [128, 1024], F32, stk),
            t2=sbuf(f"qt2{tag}", [128, 1024], F32, stk),
            ti=sbuf(f"qti{tag}", [128, 1024], I32, stk),
            z=sbuf(f"qz{tag}", [128, 2048], F32, stk),
            az=sbuf(f"qaz{tag}", [128, 2048], F32, stk),
            L=sbuf(f"qL{tag}", [128, 1024], I32, stk),
            G=sbuf(f"qG{tag}", [128, 1024], I32, stk),
            F=sbuf(f"qF{tag}", [128, 1024], I32, stk),
            U=sbuf(f"qU{tag}", [128, 1024], F32, stk),
            U2=sbuf(f"qU2{tag}", [128, 1024], BF16, stk),
            FR=sbuf(f"qFR{tag}", [128, 1024], F32, stk),
            Q=sbuf(f"qQ{tag}", [128, 1024], BF16, stk),
            S=sbuf(f"qS{tag}", [128, 2048], BF16, stk),
            X=sbuf(f"qX{tag}", [128, 2048], BF16, stk),
            _stx=[0, 0],
            _vgf=[0],
        )

    def scale_chain(qb, Fb, gs_col):
        amax, sc8h, grep = qb["amax"], qb["sc8h"], qb["grep"]
        t1, t2, ti = qb["t1"], qb["t2"], qb["ti"]
        s_v.inc(VE.tensor_scalar(col(19), C[:, gs_col:gs_col + 1], 1.0 / 6.0, None,
                                 ALU.mult))
        vsync()
        s_v.inc(VE.tensor_scalar(t1[:, 0:Fb], amax[:, 0:Fb], col(19), 448.0,
                                 ALU.mult, ALU.min))
        s_v.wt(SE)
        s_s.inc(SE.activation(t2[:, 0:Fb], t1[:, 0:Fb], ACT.Ln))
        s_s.wt(VE)
        s_v.inc(VE.tensor_scalar(t2[:, 0:Fb], t2[:, 0:Fb], 1.0 / LN2C, -0.5,
                                 ALU.mult, ALU.add))
        s_v.inc(VE.tensor_scalar(ti[:, 0:Fb], t2[:, 0:Fb], -6.0, None, ALU.max))
        e8i = qb["Q"]
        e8b = qb["S"]
        s_v.wt(SE)
        s_s.inc(SE.activation(e8i[:, 0:Fb], ti[:, 0:Fb], ACT.Exp,
                              bias=ccol(3.0 * LN2C), scale=ccol(-LN2C)))
        s_s.inc(SE.activation(e8b[:, 0:Fb], ti[:, 0:Fb], ACT.Exp,
                              bias=ccol(-4.0 * LN2C), scale=ccol(LN2C)))
        s_s.wt(VE)
        s_v.inc(VE.tensor_tensor(t1[:, 0:Fb], t1[:, 0:Fb], e8i[:, 0:Fb], ALU.mult))
        s_v.inc(VE.tensor_scalar(ti[:, 0:Fb], t1[:, 0:Fb], 1.0, None, ALU.mult))
        s_v.inc(VE.tensor_tensor(sc8h[:, 0:Fb], ti[:, 0:Fb], e8b[:, 0:Fb], ALU.mult))
        s_v.inc(VE.tensor_scalar(t2[:, 0:Fb], sc8h[:, 0:Fb], 2.0, 5e-7,
                                 ALU.mult, ALU.max))
        s_v.inc(VE.reciprocal(t1[:, 0:Fb], t2[:, 0:Fb]))
        s_v.inc(VE.tensor_scalar(grep[:, 0:Fb], t1[:, 0:Fb],
                                 C[:, gs_col:gs_col + 1], None, ALU.mult))

    def quant_s1(qb, src, Fn, par):
        z = qb["z"][:, par * 1024:par * 1024 + Fn]
        az = qb["az"][:, par * 1024:par * 1024 + Fn]
        Sg = qb["S"][:, par * 1024:par * 1024 + Fn]
        nb = Fn // 16
        blk_off = qb["_s1off"]
        grepap = qb["grep"][:, blk_off:blk_off + nb].rearrange(
            "p (b o) -> p b o", o=1).broadcast_to([128, nb, 16])
        s_v.inc(VE.tensor_tensor(z, src, grepap, ALU.mult))
        s_v.wt(SE)
        s_s.inc(SE.activation(az, z, ACT.Abs))
        s_s.inc(SE.activation(Sg, z, ACT.Sign))
        return s_s.n

    def quant_s2(qb, blk_off, Fn, par, dst_ap, store_sem, s_cnt):
        nb = Fn // 16
        sc8ap = qb["sc8h"][:, blk_off:blk_off + nb].rearrange(
            "p (b o) -> p b o", o=1).broadcast_to([128, nb, 16])
        az = qb["az"][:, par * 1024:par * 1024 + Fn]
        Sg = qb["S"][:, par * 1024:par * 1024 + Fn]
        Xs = qb["X"][:, par * 1024:par * 1024 + Fn]
        L, G, F = qb["L"][:, 0:Fn], qb["G"][:, 0:Fn], qb["F"][:, 0:Fn]
        GR, FR = qb["U"][:, 0:Fn], qb["FR"][:, 0:Fn]
        U2, Q = qb["U2"][:, 0:Fn], qb["Q"][:, 0:Fn]
        t1 = qb["t1"][:, 0:Fn]
        t2 = qb["t2"][:, 0:Fn]
        s_v.wt(SE, qb["_vgf"][0])
        s_s.inc(SE.activation(GR, az, ACT.Relu, bias=ccol(-1.9999999)))
        s_s.inc(SE.activation(FR, az, ACT.Relu, bias=ccol(-4.4999995)))
        s_gf = s_s.n
        s_s.wt(VE, s_cnt)
        s_v.inc(VE.tensor_scalar(L, az, 2.0, 4.0, ALU.mult, ALU.min))
        s_v.inc(VE.scalar_tensor_tensor(t2, az, 0.25, L, ALU.is_equal, ALU.add))
        s_v.inc(VE.scalar_tensor_tensor(t1, az, 1.25, t2, ALU.is_equal, ALU.add))
        s_s.wt(VE, s_gf)
        s_v.inc(VE.tensor_scalar(G, GR, 2.0, None, ALU.min))
        s_v.inc(VE.tensor_scalar(F, FR, 1.0, None, ALU.min))
        qb["_vgf"][0] = s_v.n
        s_v.inc(VE.scalar_tensor_tensor(U2, G, 2.0, t1, ALU.mult, ALU.add))
        s_v.inc(VE.scalar_tensor_tensor(Q, F, 4.0, U2, ALU.mult, ALU.add))
        s_v.inc(VE.tensor_tensor(U2, Q, sc8ap, ALU.mult))
        store_sem.wt(VE, qb["_stx"][par])
        s_v.inc(VE.tensor_tensor(Xs, U2, Sg, ALU.mult))
        s_v.wt(GE)
        store_sem.d(GE.dma_start(dst_ap, Xs))
        qb["_stx"][par] = store_sem.n

    def quant_tiles(qb, tiles, store_sem):
        pend = None
        for i, (src, blk_off, Fn, dst) in enumerate(tiles):
            qb["_s1off"] = blk_off
            cnt = quant_s1(qb, src, Fn, i % 2)
            if pend is not None:
                quant_s2(qb, *pend)
            pend = (blk_off, Fn, i % 2, dst, store_sem, cnt)
        if pend is not None:
            quant_s2(qb, *pend)

    s_ld.d(SY.dma_start(decay_t[:].rearrange("p (j d) -> p j d", d=8), alog_d[:, :].rearrange("(j p) d -> p j d", p=128)))
    s_ld.d(SY.dma_start(bpos_t[:].rearrange("p (j d) -> p j d", d=8), bmat_d[:, :].rearrange("(j p) d -> p j d", p=128)))
    s_ld.d(SY.dma_start(wyc_t[:].rearrange("p (j d) -> p j d", d=8), cmat_d[:, :].rearrange("(j p) d -> p j d", p=128)))
    with nc.allow_non_contiguous_dma(reason="tiny per-k-tile gain/bias column loads"):
        s_ld.d(SY.dma_start(gcol_t[:], lng_d[0:1, :].rearrange("o (j p) -> p o j", p=128).squeeze(1)))
        s_ld.d(SY.dma_start(bcol_t[:], lnb_d[0:1, :].rearrange("o (j p) -> p o j", p=128).squeeze(1)))
    with nc.allow_non_contiguous_dma(reason="scalar broadcasts"):
        s_ld.d(SY.dma_start(C[:, 0:1], sin_d[0:1, 0:1].broadcast_to([128, 1])))
        s_ld.d(SY.dma_start(C[:, 1:2], sind_d[0:1, 0:1].broadcast_to([128, 1])))
    s_ld.d(SY.dma_start(g2rep[:], lng2_d[0:1, :].broadcast_to([128, D])))
    s_ld.d(SY.dma_start(b2rep[:], lnb2_d[0:1, :].broadcast_to([128, D])))
    ld0 = s_ld.n
    s_ld.wt(SE, ld0)
    s_s.inc(SE.activation(col(2), col(0), ACT.Ln))
    s_s.inc(SE.activation(col(3), col(1), ACT.Ln))

    s_ld.wt(VE, ld0)
    s_v.inc(VE.tensor_scalar(carry_t[:], decay_t[:], 0.0, -5.0, ALU.min, ALU.max))
    s_v.wt(SE)
    s_s.inc(SE.activation(carry_t[:], carry_t[:], ACT.Exp))
    s_s.wt(VE)
    s_v.inc(VE.tensor_scalar(carry_t[:], carry_t[:], 0.01, 2.0, ALU.max, ALU.min))
    vsync()
    s_v.inc(VE.tensor_scalar(decay_t[:], carry_t[:], -0.9, None, ALU.mult))
    s_v.inc(VE.tensor_tensor(carry_t[:], bpos_t[:], wyc_t[:], ALU.mult))
    vsync()
    s_v.inc(VE.tensor_scalar(wyc_t[:], carry_t[:], 0.1, None, ALU.mult))
    s_v.wt(SE)
    s_s.inc(SE.activation(carry_t[:], bpos_t[:], ACT.Abs))
    s_s.wt(VE)
    s_v.inc(VE.tensor_scalar(carry_t[:], carry_t[:], 1e-30, None, ALU.max))
    vsync()
    s_v.inc(VE.reciprocal(bpos_t[:], carry_t[:]))
    vsync()
    s_v.inc(VE.tensor_scalar(bpos_t[:], bpos_t[:], 50.0, None, ALU.mult))
    vsync()
    s_v.inc(VE.tensor_scalar(bneg_t[:], bpos_t[:], -1.0, None, ALU.mult))
    s_v.inc(VE.memset(carry_t[:], 0.0))

    with ExitStack() as sk:
        sxn = sbuf("sxn", [128, NT * D], F32, sk)
        mv6b = sbuf("mv6b", [128, NT * 12], F32, sk)
        mvag = sbuf("mvag", [128, NT * 2], F32, sk)
        lnc = sbuf("lnc", [128, 80], F32, sk)
        qb = make_qbufs(sk, "a")
        ldm = []
        for h in range(4):
            s_ld.d(SY.dma_start(
                sxn[:, h * 4 * D:(h + 1) * 4 * D].rearrange("p (n d) -> p n d", d=D),
                x_d[h * 512:(h + 1) * 512, :].rearrange("(n p) d -> p n d", p=128)))
            ldm.append(s_ld.n)

        def ln_batch(apply_marks=None):
            for n in range(NT):
                if apply_marks is None:
                    s_ld.wt(VE, ldm[n // 4])
                else:
                    s_s.wt(VE, apply_marks[n])
                xt = sxn[:, n * D:(n + 1) * D]
                s_v.inc(VE.bn_stats(mv6b[:, n * 12:n * 12 + 6], xt[:, 0:512]))
                s_v.inc(VE.bn_stats(mv6b[:, n * 12 + 6:n * 12 + 12], xt[:, 512:1024]))
            vsync()
            for n in range(NT):
                s_v.inc(VE.bn_aggr(mvag[:, n * 2:(n + 1) * 2],
                                   mv6b[:, n * 12:(n + 1) * 12]))
            vsync()
            mvv = mvag[:].rearrange("p (n k) -> p n k", k=2)
            vw = mvv[:, :, 1:2]
            mw = mvv[:, :, 0:1]
            c3 = lambda a, b: lnc[:, a:b].rearrange("p (n k) -> p n k", k=1)
            s_s.wt(VE)
            s_v.inc(VE.tensor_scalar(c3(0, 16), vw, 1e-5, None, ALU.add))
            vsync()
            s_v.wt(SE)
            s_s.inc(SE.activation(lnc[:, 16:32], lnc[:, 0:16], ACT.Sqrt))
            s_s.wt(VE)
            s_v.inc(VE.reciprocal(lnc[:, 32:48], lnc[:, 16:32]))
            vsync()
            s_v.inc(VE.tensor_tensor(lnc[:, 48:64], lnc[:, 32:48], lnc[:, 32:48],
                                     ALU.mult))
            vsync()
            s_v.inc(VE.tensor_tensor(lnc[:, 48:64], lnc[:, 0:16], lnc[:, 48:64],
                                     ALU.mult))
            vsync()
            s_v.inc(VE.tensor_scalar(lnc[:, 48:64], lnc[:, 48:64], -0.5, 1.5,
                                     ALU.mult, ALU.add))
            vsync()
            s_v.inc(VE.tensor_tensor(lnc[:, 64:80], lnc[:, 32:48], lnc[:, 48:64],
                                     ALU.mult))
            vsync()
            s_v.inc(VE.tensor_tensor(c3(48, 64), mw, c3(64, 80), ALU.mult))
            vsync()
            s_v.inc(VE.tensor_scalar(lnc[:, 48:64], lnc[:, 48:64], -1.0, None,
                                     ALU.mult))
            vsync()

        ln_batch()
        marks1 = []
        s_v.wt(SE)
        for n in range(NT):
            xt = sxn[:, n * D:(n + 1) * D]
            s_s.inc(SE.activation(xt, xt, ACT.Identity,
                                  bias=lnc[:, 48 + n:49 + n],
                                  scale=lnc[:, 64 + n:65 + n]))
            marks1.append(s_s.n)
        ln_batch(apply_marks=marks1)
        for n in range(NT):
            xb = sxn[:, n * D:(n + 1) * D].rearrange("p (b k) -> p b k", k=16)
            s_v.inc(VE.tensor_reduce(qb["amax"][:, n * 64:(n + 1) * 64], xb, AXX,
                                     ALU.max, apply_absolute_value=True))
        scale_chain(qb, NT * 64, 0)
        s_v.wt(SE)
        stq = [s_gd.n, s_gd.n]
        for n in range(NT):
            par = n % 2
            xt = sxn[:, n * D:(n + 1) * D]
            Xst = qb["X"][:, par * 1024:par * 1024 + D]
            s_gd.wt(SE, stq[par])
            s_s.inc(SE.activation(Xst, xt, ACT.Identity,
                                  bias=lnc[:, 48 + n:49 + n],
                                  scale=lnc[:, 64 + n:65 + n]))
            s_s.wt(GE)
            s_gd.d(GE.dma_start(xn2_dd[n * 128:(n + 1) * 128, :], Xst))
            stq[par] = s_gd.n
        qb["_stx"] = stq
        quant_tiles(qb, [(sxn[:, n * D:(n + 1) * D], n * 64, D,
                          xq_dd[n * 128:(n + 1) * 128, :]) for n in range(NT)],
                    s_gd)

    ska = ExitStack()
    winT_bf = sbuf("winT_bf", [128, 8 * I3], BF16, ska)
    woutT_bf = sbuf("woutT_bf", [128, 12 * D], BF16, ska)
    with ExitStack() as sk:
        wtmp = sbuf("wtmp", [128, 2 * I3], F32, sk)
        row1 = sbuf("row1", [1, I3], F32, sk)
        pbias = [sk.enter_context(nc.psum_tensor(f"pbias{i}", [1, 512], F32)) for i in range(6)]
        s_v.wt(SY)
        s_s.wt(SY)
        s_gd.wt(SY)
        s_gd.d(GE.dma_start(woutT_bf[:].rearrange("p (j d) -> p j d", d=D),
                            woutT_d[:, :].rearrange("(j p) d -> p j d", p=128)))
        for kj in range(8):
            buf = wtmp[:, (kj % 2) * I3:(kj % 2) * I3 + I3]
            if kj >= 2:
                s_s.wt(SY)
                s_t.wt(SY)
            s_ld.d(SY.dma_start(buf, winT_d[kj * 128:(kj + 1) * 128, :]))
            s_ld.wt(SE)
            s_s.inc(SE.activation(winT_bf[:, kj * I3:(kj + 1) * I3], buf, ACT.Copy,
                                  scale=gcol_t[:, kj:kj + 1]))
            s_ld.wt(TE)
            for c in range(6):
                ins = TE.matmul(pbias[c][:], bcol_t[:, kj:kj + 1],
                                buf[:, c * 512:(c + 1) * 512],
                                start=(kj == 0), stop=(kj == 7))
            s_t.inc(ins)
        s_t.wt(VE)
        for c in range(6):
            s_v.inc(VE.tensor_copy(row1[0:1, c * 512:(c + 1) * 512], pbias[c][:]))
        s_v.wt(GE)
        s_gd.d(GE.dma_start(bias_dd[0:1, :], row1[:]))
        s_gd.wt(SY)
        with nc.allow_non_contiguous_dma(reason="tiny bias column reload"):
            s_ld.d(SY.dma_start(bias_col[:, 0:24],
                                bias_dd[0:1, :].rearrange("o (c p) -> p o c", p=128).squeeze(1)))
    ld_bias = s_ld.n

    with ExitStack() as sk:
        wq = sbuf("wq", [128, 4 * D], F32, sk)
        qb = make_qbufs(sk, "w")
        srcs = [(gslice_d, 1), (uslice_d, 1), (dslice_d, None)]
        s_s.wt(SY)
        s_t.wt(SY)
        for wi, (src_d, pat) in enumerate(srcs):
            s_v.wt(SY)
            if pat:
                s_ld.d(SY.dma_start(wq[:].rearrange("p (j d) -> p j d", d=D),
                                    src_d[:, :].rearrange("(j p) d -> p j d", p=128)))
            else:
                s_ld.d(SY.dma_start(wq[:], src_d[:, :]))
            s_ld.wt(VE)
            wb = wq[:].rearrange("p (b k) -> p b k", k=16)
            s_v.inc(VE.tensor_reduce(qb["amax"][:, 0:256], wb, AXX, ALU.max,
                                     apply_absolute_value=True))
            s_v.inc(VE.tensor_reduce(col(4 + wi), qb["amax"][:, 0:256], AXX, ALU.max))
        s_v.wt(GE)
        s_gd.d(GE.dma_start(gmx_dd[:, :], C[:, 4:8]))
        s_gd.wt(SY)
        with nc.allow_non_contiguous_dma(reason="gmax transpose readback"):
            for wi in range(3):
                s_ld.d(SY.dma_start(qb["t1"][0:1, wi * 128:(wi + 1) * 128],
                                    gmx_dd[:, wi:wi + 1].transpose([1, 0])))
        s_ld.wt(VE)
        s_v.inc(VE.memset(qb["t2"][0:1, 0:8], 0.0))
        for wi in range(3):
            s_v.inc(VE.tensor_reduce(qb["t2"][0:1, wi:wi + 1],
                                     qb["t1"][0:1, wi * 128:(wi + 1) * 128],
                                     AXX, ALU.max))
        s_v.wt(GE)
        s_gd.d(GE.dma_start(am8_in[0:1, :], qb["t2"][0:1, 0:8]))
        s_gd.wt(GE)
        s_g.inc(GE.collective_compute("AllReduce", ALU.max, rg,
                                      [am8_in[:, :]], [am8_out[:, :]]))
        s_g.wt(SY)
        with nc.allow_non_contiguous_dma(reason="gmax broadcast"):
            for wi in range(3):
                s_ld.d(SY.dma_start(C[:, 4 + wi:5 + wi],
                                    am8_out[0:1, wi:wi + 1].broadcast_to([128, 1])))
        s_ld.wt(VE)
        for wi in range(3):
            s_v.inc(VE.reciprocal(col(20 + wi), col(4 + wi)))
        vsync()
        for wi in range(3):
            s_v.inc(VE.tensor_scalar(col(7 + wi), col(20 + wi), 2688.0, None, ALU.mult))
        vsync()
        s_v.wt(SE)
        for wi in range(3):
            s_s.inc(SE.activation(col(10 + wi), col(7 + wi), ACT.Ln))
        s_v.inc(VE.tensor_tensor(col(20), col(0), col(7), ALU.mult))
        s_v.inc(VE.tensor_tensor(col(21), col(0), col(8), ALU.mult))
        s_v.inc(VE.tensor_tensor(col(22), col(1), col(9), ALU.mult))
        vsync()
        s_v.inc(VE.reciprocal(col(13), col(20)))
        s_v.inc(VE.reciprocal(col(14), col(21)))
        s_v.inc(VE.reciprocal(col(15), col(22)))
        vsync()
        dsts = [wsg_sl, wsu_sl, wsd_sl]
        for wi, (src_d, pat) in enumerate(srcs):
            s_v.wt(SY)
            s_s.wt(SY)
            if pat:
                s_ld.d(SY.dma_start(wq[:].rearrange("p (j d) -> p j d", d=D),
                                    src_d[:, :].rearrange("(j p) d -> p j d", p=128)))
            else:
                s_ld.d(SY.dma_start(wq[:], src_d[:, :]))
            s_ld.wt(VE)
            wb = wq[:].rearrange("p (b k) -> p b k", k=16)
            s_v.inc(VE.tensor_reduce(qb["amax"][:, 0:256], wb, AXX, ALU.max,
                                     apply_absolute_value=True))
            scale_chain(qb, 256, 7 + wi)
            tiles = []
            for tt in range(4):
                if pat:
                    dap = dsts[wi][tt * 128:(tt + 1) * 128, :]
                else:
                    dap = dsts[wi][:, tt * D:(tt + 1) * D]
                tiles.append((wq[:, tt * D:(tt + 1) * D], tt * 64, D, dap))
            quant_tiles(qb, tiles, s_gd)
        s_gd.wt(GE)
        s_g.inc(GE.collective_compute("AllGather", ALU.bypass, rg,
                                      [wsg_sl[:, :]], [wsg_f[:, :]]))
        s_g.inc(GE.collective_compute("AllGather", ALU.bypass, rg,
                                      [wsu_sl[:, :]], [wsu_f[:, :]]))
        s_g.inc(GE.collective_compute("AllGather", ALU.bypass, rg,
                                      [wsd_sl[:, :]], [wsd_f[:, :]]))
    g_ws = s_g.n

    pb = [st.enter_context(nc.psum_tensor(f"pb{i}", [128, 512], F32)) for i in range(6)]
    ev = [(None, 0)] * 6

    with ExitStack() as sk:
        xn2T = sbuf("xn2T", [128, 8 * GT], BF16, sk)
        xssm = sbuf("xssm", [128, 12 * GT], F32, sk)
        tanhg = sbuf("tanhg", [128, 12 * GT], F32, sk)
        ybuf = sbuf("ybuf", [128, 12 * GT], F32, sk)
        gatedb = sbuf("gatedb", [128, 12 * GT], BF16, sk)
        htmp = sbuf("htmp", [128, 3 * GT], F32, sk)
        attn = sbuf("attn", [128, D], F32, sk)
        z1b = sbuf("z1b", [128, D], F32, sk)

        for qe in (SY, SE):
            s_v.wt(qe)
            s_gd.wt(qe)
        s_s.wt(SY)
        for gg in range(NG):
            s_t.wt(SY)
            s_t.wt(SE)
            for kj in range(8):
                s_sc.d(SY.dma_start(
                    xn2T[:, kj * GT:(kj + 1) * GT],
                    xn2_dd[gg * GT:(gg + 1) * GT, kj * 128:(kj + 1) * 128],
                    transpose=True))
            sc_x = s_sc.n
            s_ld.wt(VE, ld_bias)
            for oc in range(24):
                bank = oc % 2
                s_sc.wt(TE, sc_x)
                sm, cnt = ev[bank]
                if sm is not None:
                    sm.wt(TE, cnt)
                for kj in range(8):
                    ins = TE.matmul(pb[bank][:],
                                    winT_bf[:, kj * I3 + oc * 128:kj * I3 + (oc + 1) * 128],
                                    xn2T[:, kj * GT:(kj + 1) * GT],
                                    start=(kj == 0), stop=(kj == 7))
                s_t.inc(ins)
                s_t.wt(VE)
                dst = (xssm if oc < 12 else tanhg)
                j = oc if oc < 12 else oc - 12
                dd = dst[:, j * GT:(j + 1) * GT]
                s_v.inc(VE.tensor_scalar(dd, pb[bank][:], bias_col[:, oc:oc + 1], 5.0,
                                         ALU.add, ALU.min))
                s_v.inc(VE.tensor_scalar(dd, dd, -5.0, None, ALU.max))
                ev[bank] = (s_v, s_v.n)
            s_v.wt(SE)
            for j in range(12):
                s_s.inc(SE.activation(tanhg[:, j * GT:(j + 1) * GT],
                                      tanhg[:, j * GT:(j + 1) * GT], ACT.Tanh))
            pend = []
            for j in range(12):
                xs_j = xssm[:, j * GT:(j + 1) * GT]
                y_j = ybuf[:, j * GT:(j + 1) * GT]
                for d in range(DS):
                    cc = j * 8 + d
                    k3 = (j * 8 + d) % 3
                    ht = htmp[:, k3 * GT:(k3 + 1) * GT]
                    dk = decay_t[:, cc:cc + 1].broadcast_to([128, GT])
                    s_v.inc(VE.tensor_tensor_scan(ht, dk, xs_j,
                                                  carry_t[:, cc:cc + 1],
                                                  ALU.mult, ALU.add))
                    s_v.inc(VE.tensor_scalar(ht, ht, bpos_t[:, cc:cc + 1],
                                             bneg_t[:, cc:cc + 1], ALU.min, ALU.max))
                    pend.append((cc, k3))
                    if len(pend) > 2:
                        pc, pk = pend.pop(0)
                        s_v.inc(VE.tensor_copy(carry_t[:, pc:pc + 1],
                                               htmp[:, pk * GT + GT - 1:pk * GT + GT]))
                    if d == 0:
                        s_v.inc(VE.tensor_scalar(y_j, ht, wyc_t[:, cc:cc + 1],
                                                 None, ALU.mult))
                    else:
                        s_v.inc(VE.scalar_tensor_tensor(y_j, ht,
                                                        wyc_t[:, cc:cc + 1], y_j,
                                                        ALU.mult, ALU.add))
            vsync()
            for pc, pk in pend:
                s_v.inc(VE.tensor_copy(carry_t[:, pc:pc + 1],
                                       htmp[:, pk * GT + GT - 1:pk * GT + GT]))
            pend = []
            s_s.wt(VE)
            s_t.wt(VE)
            for j in range(12):
                y_j = ybuf[:, j * GT:(j + 1) * GT]
                s_v.inc(VE.tensor_scalar(y_j, y_j, 5.0, -5.0, ALU.min, ALU.max))
                s_v.inc(VE.tensor_tensor(gatedb[:, j * GT:(j + 1) * GT], y_j,
                                         tanhg[:, j * GT:(j + 1) * GT], ALU.mult))
            v_gate = s_v.n
            for tc4 in range(4):
                tglob = gg * 4 + tc4
                for oh in range(2):
                    bank = 2 + oh
                    s_v.wt(TE, v_gate)
                    sm, cnt = ev[bank]
                    if sm is not None:
                        sm.wt(TE, cnt)
                    for kj in range(12):
                        ins = TE.matmul(
                            pb[bank][:],
                            gatedb[:, kj * GT + tc4 * 128:kj * GT + (tc4 + 1) * 128],
                            woutT_bf[:, kj * D + oh * 512:kj * D + (oh + 1) * 512],
                            start=(kj == 0), stop=(kj == 11))
                    s_t.inc(ins)
                    s_t.wt(VE)
                    s_gd.wt(VE)
                    s_v.inc(VE.tensor_scalar(attn[:, oh * 512:(oh + 1) * 512],
                                             pb[bank][:], 5.0, -5.0, ALU.min, ALU.max))
                    ev[bank] = (s_v, s_v.n)
                ln_stats(attn, 20, 21)
                s_v.wt(SE)
                s_gd.wt(SE)
                s_s.inc(SE.activation(z1b[:], attn[:], ACT.Identity, bias=col(21),
                                      scale=col(20)))
                s_s.wt(VE)
                s_v.inc(VE.tensor_tensor(z1b[:], z1b[:], g2rep[:], ALU.mult))
                s_v.inc(VE.tensor_tensor(z1b[:], z1b[:], b2rep[:], ALU.add))
                s_v.wt(GE)
                s_gd.d(GE.dma_start(s_dd[tglob * 128:(tglob + 1) * 128, :], z1b[:]))
    ska.close()

    with ExitStack() as sk:
        xqT = sbuf("xqT", [128, 8 * T], BF16, sk)
        wgT = sbuf("wgT", [128, 8 * 512], BF16, sk)
        wuT = sbuf("wuT", [128, 8 * 512], BF16, sk)
        hidc = sbuf("hidc", [128, NT * 512], F32, sk)
        silu_g = sbuf("silu_g", [128, 2 * 512], F32, sk)
        qb = make_qbufs(sk, "h")

        for qe in (SY, SE):
            s_v.wt(qe)
            s_t.wt(qe)
            s_gd.wt(qe)
            s_g.wt(qe, g_ws)
        s_s.wt(SY)
        for kj in range(8):
            s_sc.d(SY.dma_start(
                xqT[:, kj * T:(kj + 1) * T],
                xq_dd[:, kj * 128:(kj + 1) * 128], transpose=True))
        prev_tiles = None
        for ic in range(8):
            s_t.wt(SY)
            s_t.wt(SE)
            for kj in range(8):
                s_sc.d(SY.dma_start(
                    wgT[:, kj * 512:(kj + 1) * 512],
                    wsg_f[ic * 512:(ic + 1) * 512, kj * 128:(kj + 1) * 128],
                    transpose=True))
                s_sc.d(SY.dma_start(
                    wuT[:, kj * 512:(kj + 1) * 512],
                    wsu_f[ic * 512:(ic + 1) * 512, kj * 128:(kj + 1) * 128],
                    transpose=True))
            sc_w = s_sc.n
            v_sg = [0, 0]
            pend = None
            for tc in range(NT):
                bg = tc % 2
                bu = 2 + tc % 2
                s_sc.wt(TE, sc_w)
                sm, cnt = ev[bg]
                if sm is not None:
                    sm.wt(TE, cnt)
                for kj in range(8):
                    ins = TE.matmul(pb[bg][:],
                                    xqT[:, kj * T + tc * 128:kj * T + (tc + 1) * 128],
                                    wgT[:, kj * 512:(kj + 1) * 512],
                                    start=(kj == 0), stop=(kj == 7))
                s_t.inc(ins)
                t_gate = s_t.n
                sm, cnt = ev[bu]
                if sm is not None:
                    sm.wt(TE, cnt)
                for kj in range(8):
                    ins = TE.matmul(pb[bu][:],
                                    xqT[:, kj * T + tc * 128:kj * T + (tc + 1) * 128],
                                    wuT[:, kj * 512:(kj + 1) * 512],
                                    start=(kj == 0), stop=(kj == 7))
                s_t.inc(ins)
                t_up = s_t.n
                if prev_tiles is not None:
                    src, boff, Fn, dst = prev_tiles[tc]
                    qb["_s1off"] = boff
                    cnt = quant_s1(qb, src, Fn, tc % 2)
                    if pend is not None:
                        quant_s2(qb, *pend)
                    pend = (boff, Fn, tc % 2, dst, s_gd, cnt)
                sg = silu_g[:, (tc % 2) * 512:(tc % 2) * 512 + 512]
                s_t.wt(SE, t_gate)
                s_v.wt(SE, v_sg[tc % 2])
                s_s.inc(SE.activation(sg, pb[bg][:], ACT.Silu, scale=col(13)))
                ev[bg] = (s_s, s_s.n)
                s_t.wt(VE, t_up)
                s_s.wt(VE)
                s_v.inc(VE.scalar_tensor_tensor(hidc[:, tc * 512:(tc + 1) * 512],
                                                pb[bu][:], col(14), sg,
                                                ALU.mult, ALU.mult))
                ev[bu] = (s_v, s_v.n)
                v_sg[tc % 2] = s_v.n
                hb = hidc[:, tc * 512:(tc + 1) * 512].rearrange("p (b k) -> p b k", k=16)
                s_v.inc(VE.tensor_reduce(qb["amax"][:, tc * 32:(tc + 1) * 32], hb, AXX,
                                         ALU.max, apply_absolute_value=True))
            if pend is not None:
                quant_s2(qb, *pend)
            scale_chain(qb, NT * 32, 1)
            prev_tiles = [(hidc[:, tc * 512:(tc + 1) * 512], tc * 32, 512,
                           hq_dd[tc * 128:(tc + 1) * 128,
                                 ic * 512:(ic + 1) * 512])
                          for tc in range(NT)]
        quant_tiles(qb, prev_tiles, s_gd)

    with ExitStack() as sk:
        TH = T // 2
        hqTh = sbuf("hqTh", [128, 32 * TH], BF16, sk)
        wdT2 = sbuf("wdT2", [128, 32 * D], BF16, sk)
        gch = sbuf("gch", [128, 2 * D], F32, sk)
        xch = sbuf("xch", [128, 2 * D], F32, sk)
        sch = sbuf("sch", [128, 2 * D], F32, sk)

        for qe in (SY, SE):
            s_v.wt(qe)
            s_t.wt(qe)
            s_gd.wt(qe)
        s_s.wt(SY)
        for q in range(32):
            s_sc.d(SY.dma_start(
                wdT2[:, q * D:(q + 1) * D],
                wsd_f[:, q * 128:(q + 1) * 128],
                transpose=True))
        v_xs = [0, 0]
        st_g = [0, 0]
        for th in range(2):
            if th == 1:
                s_t.wt(SY)
                s_t.wt(SE)
            for q in range(32):
                s_sc.d(SY.dma_start(
                    hqTh[:, q * TH:(q + 1) * TH],
                    hq_dd[th * TH:(th + 1) * TH, q * 128:(q + 1) * 128],
                    transpose=True))
            sc_h = s_sc.n
            for tci in range(8):
                tc = th * 8 + tci
                par = tc % 2
                s_v.wt(SY, v_xs[par])
                s_ld.d(SY.dma_start(xch[:, par * D:(par + 1) * D],
                                    x_d[tc * 128:(tc + 1) * 128, :]))
                s_ld.d(SY.dma_start(sch[:, par * D:(par + 1) * D],
                                    s_dd[tc * 128:(tc + 1) * 128, :]))
                ldn = s_ld.n
                t_dn = [0, 0]
                for oh in range(2):
                    bank = par * 2 + oh
                    s_sc.wt(TE, sc_h)
                    sm, cnt = ev[bank]
                    if sm is not None:
                        sm.wt(TE, cnt)
                    for q in range(32):
                        ins = TE.matmul(
                            pb[bank][:],
                            hqTh[:, q * TH + tci * 128:q * TH + (tci + 1) * 128],
                            wdT2[:, q * D + oh * 512:q * D + (oh + 1) * 512],
                            start=(q == 0), stop=(q == 31))
                    s_t.inc(ins)
                    t_dn[oh] = s_t.n
                g = gch[:, par * D:(par + 1) * D]
                s_st.wt(SE, st_g[par])
                for oh in range(2):
                    s_t.wt(SE, t_dn[oh])
                    s_s.inc(SE.activation(g[:, oh * 512:(oh + 1) * 512],
                                          pb[par * 2 + oh][:], ACT.Silu,
                                          scale=col(15)))
                    ev[par * 2 + oh] = (s_s, s_s.n)
                s_s.wt(VE)
                s_ld.wt(VE, ldn)
                s_v.inc(VE.tensor_tensor(g, g, sch[:, par * D:(par + 1) * D],
                                         ALU.mult))
                s_v.inc(VE.tensor_tensor(g, g, xch[:, par * D:(par + 1) * D],
                                         ALU.add))
                v_xs[par] = s_v.n
                s_v.wt(GE)
                s_st.d(GE.dma_start(out_d[tc * 128:(tc + 1) * 128, :], g))
                st_g[par] = s_st.n
        s_st.wt(GE)

    return nc


_NC = None


def kernel(L0_attn_x, L0_attn_W_in, L0_attn_W_out, L0_attn_A_log, L0_attn_B,
           L0_attn_C, L0_attn_ln_in_g, L0_attn_ln_in_b, L0_attn_ln_out_g,
           L0_attn_ln_out_b, L0_ffn_gate_w, L0_ffn_up_w, L0_ffn_down_w,
           L0_ffn_s_in, L0_ffn_s_in_down):
    global _NC
    x = np.ascontiguousarray(np.asarray(L0_attn_x, dtype=np.float32))
    winT = np.ascontiguousarray(np.asarray(L0_attn_W_in, dtype=np.float32).T)
    woutT = np.ascontiguousarray(np.asarray(L0_attn_W_out, dtype=np.float32).T)
    gw = np.ascontiguousarray(np.asarray(L0_ffn_gate_w, dtype=np.float32))
    uw = np.ascontiguousarray(np.asarray(L0_ffn_up_w, dtype=np.float32))
    dw = np.ascontiguousarray(np.asarray(L0_ffn_down_w, dtype=np.float32))
    alog = np.ascontiguousarray(np.asarray(L0_attn_A_log, dtype=np.float32))
    bmat = np.ascontiguousarray(np.asarray(L0_attn_B, dtype=np.float32))
    cmat = np.ascontiguousarray(np.asarray(L0_attn_C, dtype=np.float32))
    lng = np.ascontiguousarray(np.asarray(L0_attn_ln_in_g, dtype=np.float32).reshape(1, D))
    lnb = np.ascontiguousarray(np.asarray(L0_attn_ln_in_b, dtype=np.float32).reshape(1, D))
    lng2 = np.ascontiguousarray(np.asarray(L0_attn_ln_out_g, dtype=np.float32).reshape(1, D))
    lnb2 = np.ascontiguousarray(np.asarray(L0_attn_ln_out_b, dtype=np.float32).reshape(1, D))
    sin = np.ascontiguousarray(np.asarray(L0_ffn_s_in, dtype=np.float32).reshape(1, 1))
    sind = np.ascontiguousarray(np.asarray(L0_ffn_s_in_down, dtype=np.float32).reshape(1, 1))

    if _NC is None:
        _NC = build_nc()
    in_maps = []
    for c in range(NCORE):
        in_maps.append({
            "x": np.ascontiguousarray(x[c]),
            "winT": winT, "woutT": woutT,
            "gate_slice": np.ascontiguousarray(gw[c * 512:(c + 1) * 512]),
            "up_slice": np.ascontiguousarray(uw[c * 512:(c + 1) * 512]),
            "down_slice": np.ascontiguousarray(dw[c * 128:(c + 1) * 128]),
            "a_log": alog, "b_mat": bmat, "c_mat": cmat,
            "ln_in_g": lng, "ln_in_b": lnb, "ln_out_g": lng2, "ln_out_b": lnb2,
            "s_in": sin, "s_in_down": sind,
        })
    res = run_bass_kernel_spmd(_NC, in_maps, core_ids=list(range(NCORE)))
    return np.stack([np.asarray(r["out"], dtype=np.float32) for r in res.results])


# version 19
# speedup vs baseline: 1.3956x; 1.0391x over previous
import numpy as np
from contextlib import ExitStack

import concourse.bass as bass
import concourse.mybir as mybir
from concourse.bass_utils import run_bass_kernel_spmd
from concourse.bass_isa_rust import ReduceOp

F32 = mybir.dt.float32
BF16 = mybir.dt.bfloat16
I32 = mybir.dt.int32
ALU = mybir.AluOpType
ACT = mybir.ActivationFunctionType
AXX = mybir.AxisListType.X

T = 2048
D = 1024
DI = 1536
DS = 8
I3 = 3072
II = 4096
NT = T // 128
NG = 4
GT = 512
NCORE = 8
LN2C = float(np.log(2.0))
LN6C = float(np.log(6.0))


def build_nc():
    nc = bass.Bass("TRN2", target_bir_lowering=False)

    x_d = nc.dram_tensor("x", [T, D], F32, kind="ExternalInput")
    winT_d = nc.dram_tensor("winT", [D, I3], F32, kind="ExternalInput")
    woutT_d = nc.dram_tensor("woutT", [DI, D], F32, kind="ExternalInput")
    gslice_d = nc.dram_tensor("gate_slice", [II // 8, D], F32, kind="ExternalInput")
    uslice_d = nc.dram_tensor("up_slice", [II // 8, D], F32, kind="ExternalInput")
    dslice_d = nc.dram_tensor("down_slice", [D // 8, II], F32, kind="ExternalInput")
    alog_d = nc.dram_tensor("a_log", [DI, DS], F32, kind="ExternalInput")
    bmat_d = nc.dram_tensor("b_mat", [DI, DS], F32, kind="ExternalInput")
    cmat_d = nc.dram_tensor("c_mat", [DI, DS], F32, kind="ExternalInput")
    lng_d = nc.dram_tensor("ln_in_g", [1, D], F32, kind="ExternalInput")
    lnb_d = nc.dram_tensor("ln_in_b", [1, D], F32, kind="ExternalInput")
    lng2_d = nc.dram_tensor("ln_out_g", [1, D], F32, kind="ExternalInput")
    lnb2_d = nc.dram_tensor("ln_out_b", [1, D], F32, kind="ExternalInput")
    sin_d = nc.dram_tensor("s_in", [1, 1], F32, kind="ExternalInput")
    sind_d = nc.dram_tensor("s_in_down", [1, 1], F32, kind="ExternalInput")
    out_d = nc.dram_tensor("out", [T, D], F32, kind="ExternalOutput")

    xq_dd = nc.dram_tensor("xq_dd", [T, D], BF16, kind="Internal")
    xn2_dd = nc.dram_tensor("xn2_dd", [T, D], BF16, kind="Internal")
    s_dd = nc.dram_tensor("s_dd", [T, D], F32, kind="Internal")
    hq_dd = nc.dram_tensor("hq_dd", [T, II], BF16, kind="Internal")
    bias_dd = nc.dram_tensor("bias_dd", [1, I3], F32, kind="Internal")
    wsg_sl = nc.dram_tensor("wsg_sl", [II // 8, D], BF16, kind="Internal")
    wsu_sl = nc.dram_tensor("wsu_sl", [II // 8, D], BF16, kind="Internal")
    wsd_sl = nc.dram_tensor("wsd_sl", [D // 8, II], BF16, kind="Internal")
    wsg_f = nc.dram_tensor("wsg_f", [II, D], BF16, kind="Internal")
    wsu_f = nc.dram_tensor("wsu_f", [II, D], BF16, kind="Internal")
    wsd_f = nc.dram_tensor("wsd_f", [D, II], BF16, kind="Internal")
    am8_in = nc.dram_tensor("am8_in", [1, 8], F32, kind="Internal")
    am8_out = nc.dram_tensor("am8_out", [1, 8], F32, kind="Internal")
    gmx_dd = nc.dram_tensor("gmx_dd", [128, 4], F32, kind="Internal")

    rg = [list(range(NCORE))]
    VE, SE, TE, GE, SY = nc.vector, nc.scalar, nc.tensor, nc.gpsimd, nc.sync

    st = ExitStack()
    nc._keep = st

    def sbuf(name, shape, dt=F32, stk=None):
        return (stk or st).enter_context(nc.sbuf_tensor(name, shape, dt))

    class Sem:
        def __init__(self, name):
            self.h = st.enter_context(nc.semaphore(name))
            self.n = 0
            self.lastw = {}

        def inc(self, instr, k=1):
            self.n += k
            instr.then_inc(self.h, k)

        def d(self, instr):
            self.inc(instr, 16)

        def wt(self, eng, n=None):
            tgt = self.n if n is None else n
            if tgt <= self.lastw.get(id(eng), -1):
                return
            self.lastw[id(eng)] = tgt
            eng.wait_ge(self.h, tgt)

    s_ld = Sem("s_ld")
    s_lx = Sem("s_lx")
    s_lr = Sem("s_lr")
    s_lb = Sem("s_lb")
    s_lw = Sem("s_lw")
    s_lf = [Sem("s_lf0"), Sem("s_lf1")]
    s_sc = Sem("s_sc0")
    s_s2 = Sem("s_sc1")
    s_gd = Sem("s_gd")
    s_gm = Sem("s_gm")
    s_gw = Sem("s_gw")
    s_gx = [Sem("s_gx0"), Sem("s_gx1")]
    s_st = [Sem("s_st0"), Sem("s_st1")]
    s_v = Sem("s_v")
    s_s = Sem("s_s")
    s_t = Sem("s_t")
    s_g = Sem("s_g")
    ALLSEMS = [s_ld, s_lx, s_lr, s_lb, s_lw, s_lf[0], s_lf[1], s_sc, s_s2,
               s_gd, s_gm, s_gw, s_gx[0], s_gx[1], s_st[0], s_st[1],
               s_v, s_s, s_t, s_g]

    def vsync():
        s_v.wt(VE)

    def barrier():
        for eng in (VE, SE, TE, GE, SY):
            for sm in ALLSEMS:
                if sm.n:
                    sm.wt(eng)

    C = sbuf("smalls", [128, 64])
    decay_t = sbuf("decay_t", [128, 96])
    bpos_t = sbuf("bpos_t", [128, 96])
    bneg_t = sbuf("bneg_t", [128, 96])
    wyc_t = sbuf("wyc_t", [128, 96])
    carry_t = sbuf("carry_t", [128, 96])
    g2rep = sbuf("g2rep", [128, D])
    b2rep = sbuf("b2rep", [128, D])
    bias_col = sbuf("bias_col", [128, 24])
    gcol_t = sbuf("gcol_t", [128, 8])
    bcol_t = sbuf("bcol_t", [128, 8])
    mv6 = sbuf("mv6", [128, 12])
    mv2 = sbuf("mv2", [128, 2])

    def col(i):
        return C[:, i:i + 1]

    _cc = {}

    def ccol(val):
        val = float(val)
        if val not in _cc:
            idx = 32 + len(_cc)
            _cc[val] = idx
            s_v.inc(VE.memset(col(idx), val))
        return col(_cc[val])

    for _v in (1.0, -1.0, -LN6C, 3.0 * LN2C, -LN2C, -4.0 * LN2C, LN2C,
               -1.9999999, -4.4999995):
        ccol(_v)

    def ln_stats(src, cr, cmr):
        s_v.inc(VE.bn_stats(mv6[:, 0:6], src[:, 0:512]))
        s_v.inc(VE.bn_stats(mv6[:, 6:12], src[:, 512:1024]))
        vsync()
        s_v.inc(VE.bn_aggr(mv2[:], mv6[:]))
        vsync()
        s_v.inc(VE.tensor_scalar(col(16), mv2[:, 1:2], 1e-5, None, ALU.add))
        s_v.wt(SE)
        s_s.inc(SE.activation(col(17), col(16), ACT.Sqrt))
        s_s.wt(VE)
        s_v.inc(VE.reciprocal(col(18), col(17)))
        vsync()
        s_v.inc(VE.tensor_tensor(col(19), col(18), col(18), ALU.mult))
        vsync()
        s_v.inc(VE.tensor_tensor(col(19), col(16), col(19), ALU.mult))
        vsync()
        s_v.inc(VE.tensor_scalar(col(19), col(19), -0.5, 1.5, ALU.mult, ALU.add))
        vsync()
        s_v.inc(VE.tensor_tensor(col(cr), col(18), col(19), ALU.mult))
        vsync()
        s_v.inc(VE.tensor_tensor(col(18), mv2[:, 0:1], col(cr), ALU.mult))
        vsync()
        s_v.inc(VE.tensor_scalar(col(cmr), col(18), -1.0, None, ALU.mult))
        vsync()

    def make_qbufs(stk, tag, w=1024):
        return dict(
            amax=sbuf(f"amax{tag}", [128, w], F32, stk),
            sc8h=sbuf(f"sc8h{tag}", [128, w], BF16, stk),
            grep=sbuf(f"grep{tag}", [128, w], F32, stk),
            t1=sbuf(f"qt1{tag}", [128, w], F32, stk),
            t2=sbuf(f"qt2{tag}", [128, w], F32, stk),
            ti=sbuf(f"qti{tag}", [128, w], I32, stk),
            z=sbuf(f"qz{tag}", [128, 2 * w], F32, stk),
            az=sbuf(f"qaz{tag}", [128, 2 * w], F32, stk),
            L=sbuf(f"qL{tag}", [128, w], I32, stk),
            G=sbuf(f"qG{tag}", [128, w], I32, stk),
            F=sbuf(f"qF{tag}", [128, w], I32, stk),
            U=sbuf(f"qU{tag}", [128, w], F32, stk),
            U2=sbuf(f"qU2{tag}", [128, w], BF16, stk),
            U3=sbuf(f"qU3{tag}", [128, w], BF16, stk),
            FR=sbuf(f"qFR{tag}", [128, w], F32, stk),
            Q=sbuf(f"qQ{tag}", [128, w], BF16, stk),
            S=sbuf(f"qS{tag}", [128, 2 * w], BF16, stk),
            X=sbuf(f"qX{tag}", [128, 2 * w], BF16, stk),
            _w=w,
            _stx=[0, 0],
            _vgf=[0],
        )

    def scale_chain(qb, Fb, gs_col):
        amax, sc8h, grep = qb["amax"], qb["sc8h"], qb["grep"]
        t1, t2, ti = qb["t1"], qb["t2"], qb["ti"]
        s_v.inc(VE.tensor_scalar(col(19), C[:, gs_col:gs_col + 1], 1.0 / 6.0, None,
                                 ALU.mult))
        vsync()
        s_v.inc(VE.tensor_scalar(t1[:, 0:Fb], amax[:, 0:Fb], col(19), 448.0,
                                 ALU.mult, ALU.min))
        s_v.wt(SE)
        s_s.inc(SE.activation(t2[:, 0:Fb], t1[:, 0:Fb], ACT.Ln))
        s_s.wt(VE)
        s_v.inc(VE.tensor_scalar(t2[:, 0:Fb], t2[:, 0:Fb], 1.0 / LN2C, -0.5,
                                 ALU.mult, ALU.add))
        s_v.inc(VE.tensor_scalar(ti[:, 0:Fb], t2[:, 0:Fb], -6.0, None, ALU.max))
        e8i = qb["Q"]
        e8b = qb["S"]
        s_v.wt(SE)
        s_s.inc(SE.activation(e8i[:, 0:Fb], ti[:, 0:Fb], ACT.Exp,
                              bias=ccol(3.0 * LN2C), scale=ccol(-LN2C)))
        s_s.inc(SE.activation(e8b[:, 0:Fb], ti[:, 0:Fb], ACT.Exp,
                              bias=ccol(-4.0 * LN2C), scale=ccol(LN2C)))
        s_s.wt(VE)
        s_v.inc(VE.tensor_tensor(t1[:, 0:Fb], t1[:, 0:Fb], e8i[:, 0:Fb], ALU.mult))
        s_v.inc(VE.tensor_scalar(ti[:, 0:Fb], t1[:, 0:Fb], 1.0, None, ALU.mult))
        s_v.inc(VE.tensor_tensor(sc8h[:, 0:Fb], ti[:, 0:Fb], e8b[:, 0:Fb], ALU.mult))
        s_v.inc(VE.tensor_scalar(t2[:, 0:Fb], sc8h[:, 0:Fb], 2.0, 5e-7,
                                 ALU.mult, ALU.max))
        s_v.inc(VE.reciprocal(t1[:, 0:Fb], t2[:, 0:Fb]))
        s_v.inc(VE.tensor_scalar(grep[:, 0:Fb], t1[:, 0:Fb],
                                 C[:, gs_col:gs_col + 1], None, ALU.mult))

    def quant_s1(qb, src, Fn, par):
        w = qb["_w"]
        z = qb["z"][:, par * w:par * w + Fn]
        az = qb["az"][:, par * w:par * w + Fn]
        Sg = qb["S"][:, par * w:par * w + Fn]
        nb = Fn // 16
        blk_off = qb["_s1off"]
        grepap = qb["grep"][:, blk_off:blk_off + nb].rearrange(
            "p (b o) -> p b o", o=1).broadcast_to([128, nb, 16])
        s_v.inc(VE.tensor_tensor(z, src, grepap, ALU.mult))
        s_v.wt(SE)
        s_s.inc(SE.activation(az, z, ACT.Abs))
        s_s.inc(SE.activation(Sg, z, ACT.Sign))
        return s_s.n

    def quant_s2(qb, blk_off, Fn, par, dst_ap, store_sem, s_cnt):
        nb = Fn // 16
        sc8ap = qb["sc8h"][:, blk_off:blk_off + nb].rearrange(
            "p (b o) -> p b o", o=1).broadcast_to([128, nb, 16])
        w = qb["_w"]
        az = qb["az"][:, par * w:par * w + Fn]
        Sg = qb["S"][:, par * w:par * w + Fn]
        Xs = qb["X"][:, par * w:par * w + Fn]
        L, G, F = qb["L"][:, 0:Fn], qb["G"][:, 0:Fn], qb["F"][:, 0:Fn]
        GR, FR = qb["U"][:, 0:Fn], qb["FR"][:, 0:Fn]
        U2, Q = qb["U2"][:, 0:Fn], qb["Q"][:, 0:Fn]
        U3 = qb["U3"][:, 0:Fn]
        t1 = qb["t1"][:, 0:Fn]
        t2 = qb["t2"][:, 0:Fn]
        s_v.wt(SE, qb["_vgf"][0])
        s_s.inc(SE.activation(GR, az, ACT.Relu, bias=ccol(-1.9999999)))
        s_s.inc(SE.activation(FR, az, ACT.Relu, bias=ccol(-4.4999995)))
        s_gf = s_s.n
        s_s.wt(VE, s_cnt)
        s_v.inc(VE.tensor_scalar(L, az, 2.0, 4.0, ALU.mult, ALU.min))
        s_v.inc(VE.scalar_tensor_tensor(t2, az, 0.25, L, ALU.is_equal, ALU.add))
        s_v.inc(VE.scalar_tensor_tensor(t1, az, 1.25, t2, ALU.is_equal, ALU.add))
        s_s.wt(VE, s_gf)
        s_v.inc(VE.tensor_scalar(G, GR, 2.0, None, ALU.min))
        s_v.inc(VE.tensor_scalar(F, FR, 1.0, None, ALU.min))
        qb["_vgf"][0] = s_v.n
        s_v.inc(VE.scalar_tensor_tensor(U2, G, 2.0, t1, ALU.mult, ALU.add))
        s_v.inc(VE.scalar_tensor_tensor(Q, F, 4.0, U2, ALU.mult, ALU.add))
        s_v.inc(VE.tensor_tensor(U3, Q, sc8ap, ALU.mult))
        s_gx[par].wt(VE, qb["_stx"][par])
        s_v.inc(VE.tensor_tensor(Xs, U3, Sg, ALU.mult))
        s_v.wt(GE)
        s_gx[par].d(GE.dma_start(dst_ap, Xs))
        qb["_stx"][par] = s_gx[par].n

    def quant_tiles(qb, tiles, store_sem):
        pend = None
        for i, (src, blk_off, Fn, dst) in enumerate(tiles):
            qb["_s1off"] = blk_off
            cnt = quant_s1(qb, src, Fn, i % 2)
            if pend is not None:
                quant_s2(qb, *pend)
            pend = (blk_off, Fn, i % 2, dst, store_sem, cnt)
        if pend is not None:
            quant_s2(qb, *pend)

    s_ld.d(SY.dma_start(decay_t[:].rearrange("p (j d) -> p j d", d=8), alog_d[:, :].rearrange("(j p) d -> p j d", p=128)))
    s_ld.d(SY.dma_start(bpos_t[:].rearrange("p (j d) -> p j d", d=8), bmat_d[:, :].rearrange("(j p) d -> p j d", p=128)))
    s_ld.d(SY.dma_start(wyc_t[:].rearrange("p (j d) -> p j d", d=8), cmat_d[:, :].rearrange("(j p) d -> p j d", p=128)))
    with nc.allow_non_contiguous_dma(reason="tiny per-k-tile gain/bias column loads"):
        s_ld.d(SY.dma_start(gcol_t[:], lng_d[0:1, :].rearrange("o (j p) -> p o j", p=128).squeeze(1)))
        s_ld.d(SY.dma_start(bcol_t[:], lnb_d[0:1, :].rearrange("o (j p) -> p o j", p=128).squeeze(1)))
    with nc.allow_non_contiguous_dma(reason="scalar broadcasts"):
        s_ld.d(SY.dma_start(C[:, 0:1], sin_d[0:1, 0:1].broadcast_to([128, 1])))
        s_ld.d(SY.dma_start(C[:, 1:2], sind_d[0:1, 0:1].broadcast_to([128, 1])))
    s_ld.d(SY.dma_start(g2rep[:], lng2_d[0:1, :].broadcast_to([128, D])))
    s_ld.d(SY.dma_start(b2rep[:], lnb2_d[0:1, :].broadcast_to([128, D])))
    ld0 = s_ld.n
    s_ld.wt(SE, ld0)
    s_s.inc(SE.activation(col(2), col(0), ACT.Ln))
    s_s.inc(SE.activation(col(3), col(1), ACT.Ln))

    s_ld.wt(VE, ld0)
    s_v.inc(VE.tensor_scalar(carry_t[:], decay_t[:], 0.0, -5.0, ALU.min, ALU.max))
    s_v.wt(SE)
    s_s.inc(SE.activation(carry_t[:], carry_t[:], ACT.Exp))
    s_s.wt(VE)
    s_v.inc(VE.tensor_scalar(carry_t[:], carry_t[:], 0.01, 2.0, ALU.max, ALU.min))
    vsync()
    s_v.inc(VE.tensor_scalar(decay_t[:], carry_t[:], -0.9, None, ALU.mult))
    vsync()
    s_v.inc(VE.tensor_tensor(carry_t[:], bpos_t[:], wyc_t[:], ALU.mult))
    vsync()
    s_v.inc(VE.tensor_scalar(wyc_t[:], carry_t[:], 0.1, None, ALU.mult))
    s_v.wt(SE)
    s_s.inc(SE.activation(carry_t[:], bpos_t[:], ACT.Abs))
    s_s.wt(VE)
    s_v.inc(VE.tensor_scalar(carry_t[:], carry_t[:], 1e-30, None, ALU.max))
    vsync()
    s_v.inc(VE.reciprocal(bpos_t[:], carry_t[:]))
    vsync()
    s_v.inc(VE.tensor_scalar(bpos_t[:], bpos_t[:], 50.0, None, ALU.mult))
    vsync()
    s_v.inc(VE.tensor_scalar(bneg_t[:], bpos_t[:], -1.0, None, ALU.mult))
    s_v.inc(VE.memset(carry_t[:], 0.0))

    with ExitStack() as sk:
        sxn = sbuf("sxn", [128, NT * D], F32, sk)
        mv6b = sbuf("mv6b", [128, NT * 12], F32, sk)
        mvag = sbuf("mvag", [128, NT * 2], F32, sk)
        lnc = sbuf("lnc", [128, 80], F32, sk)
        g1rep = sbuf("g1rep", [128, D], F32, sk)
        qb = make_qbufs(sk, "a")
        for h in range(4):
            s_lx.d(SY.dma_start(
                sxn[:, h * 4 * D:(h + 1) * 4 * D].rearrange("p (n d) -> p n d", d=D),
                x_d[h * 512:(h + 1) * 512, :].rearrange("(n p) d -> p n d", p=128)))
        s_lx.d(SY.dma_start(g1rep[:], lng_d[0:1, :].broadcast_to([128, D])))
        ld_g1 = s_lx.n

        def ln_batch(apply_marks=None):
            for n in range(NT):
                if apply_marks is None:
                    s_lx.wt(VE, ld_g1)
                else:
                    s_s.wt(VE, apply_marks[n])
                xt = sxn[:, n * D:(n + 1) * D]
                s_v.inc(VE.bn_stats(mv6b[:, n * 12:n * 12 + 6], xt[:, 0:512]))
                s_v.inc(VE.bn_stats(mv6b[:, n * 12 + 6:n * 12 + 12], xt[:, 512:1024]))
            vsync()
            for n in range(NT):
                s_v.inc(VE.bn_aggr(mvag[:, n * 2:(n + 1) * 2],
                                   mv6b[:, n * 12:(n + 1) * 12]))
            vsync()
            mvv = mvag[:].rearrange("p (n k) -> p n k", k=2)
            vw = mvv[:, :, 1:2]
            mw = mvv[:, :, 0:1]
            c3 = lambda a, b: lnc[:, a:b].rearrange("p (n k) -> p n k", k=1)
            s_s.wt(VE)
            s_v.inc(VE.tensor_scalar(c3(0, 16), vw, 1e-5, None, ALU.add))
            vsync()
            s_v.wt(SE)
            s_s.inc(SE.activation(lnc[:, 16:32], lnc[:, 0:16], ACT.Sqrt))
            s_s.wt(VE)
            s_v.inc(VE.reciprocal(lnc[:, 32:48], lnc[:, 16:32]))
            vsync()
            s_v.inc(VE.tensor_tensor(lnc[:, 48:64], lnc[:, 32:48], lnc[:, 32:48],
                                     ALU.mult))
            vsync()
            s_v.inc(VE.tensor_tensor(lnc[:, 48:64], lnc[:, 0:16], lnc[:, 48:64],
                                     ALU.mult))
            vsync()
            s_v.inc(VE.tensor_scalar(lnc[:, 48:64], lnc[:, 48:64], -0.5, 1.5,
                                     ALU.mult, ALU.add))
            vsync()
            s_v.inc(VE.tensor_tensor(lnc[:, 64:80], lnc[:, 32:48], lnc[:, 48:64],
                                     ALU.mult))
            vsync()
            s_v.inc(VE.tensor_tensor(c3(48, 64), mw, c3(64, 80), ALU.mult))
            vsync()
            s_v.inc(VE.tensor_scalar(lnc[:, 48:64], lnc[:, 48:64], -1.0, None,
                                     ALU.mult))
            vsync()

        wqp = sbuf("wqp", [128, 4 * D], F32, sk)
        gds = []
        srcs = [(gslice_d, 1), (uslice_d, 1), (dslice_d, None)]
        ln_batch()
        marks1 = []
        s_v.wt(SE)
        for n in range(NT):
            xt = sxn[:, n * D:(n + 1) * D]
            s_s.inc(SE.activation(xt, xt, ACT.Identity,
                                  bias=lnc[:, 48 + n:49 + n],
                                  scale=lnc[:, 64 + n:65 + n]))
            marks1.append(s_s.n)
        for wi, (src_d, pat) in enumerate(srcs):
            s_v.wt(GE)
            if pat:
                s_gd.d(GE.dma_start(wqp[:].rearrange("p (j d) -> p j d", d=D),
                                    src_d[:, :].rearrange("(j p) d -> p j d", p=128)))
            else:
                s_gd.d(GE.dma_start(wqp[:], src_d[:, :]))
            s_gd.wt(VE)
            wb = wqp[:].rearrange("p (b k) -> p b k", k=16)
            s_v.inc(VE.tensor_reduce(qb["t1"][:, 0:256], wb, AXX, ALU.max,
                                     apply_absolute_value=True))
            s_v.inc(VE.tensor_reduce(col(4 + wi), qb["t1"][:, 0:256], AXX, ALU.max))
        s_v.wt(GE)
        s_gm.d(GE.dma_start(gmx_dd[:, :], C[:, 4:8]))
        s_gm.wt(SY)
        with nc.allow_non_contiguous_dma(reason="gmax transpose readback"):
            for wi in range(3):
                s_lr.d(SY.dma_start(qb["t1"][0:1, wi * 128:(wi + 1) * 128],
                                    gmx_dd[:, wi:wi + 1].transpose([1, 0])))
        s_lr.wt(VE)
        s_v.inc(VE.memset(qb["t2"][0:1, 0:8], 0.0))
        for wi in range(3):
            s_v.inc(VE.tensor_reduce(qb["t2"][0:1, wi:wi + 1],
                                     qb["t1"][0:1, wi * 128:(wi + 1) * 128],
                                     AXX, ALU.max))
        s_v.wt(GE)
        s_gm.d(GE.dma_start(am8_in[0:1, :], qb["t2"][0:1, 0:8]))
        s_gm.wt(GE)
        s_g.inc(GE.collective_compute("AllReduce", ALU.max, rg,
                                      [am8_in[:, :]], [am8_out[:, :]]))
        ln_batch(apply_marks=marks1)
        for n in range(NT):
            xb = sxn[:, n * D:(n + 1) * D].rearrange("p (b k) -> p b k", k=16)
            s_v.inc(VE.tensor_reduce(qb["amax"][:, n * 64:(n + 1) * 64], xb, AXX,
                                     ALU.max, apply_absolute_value=True))
        scale_chain(qb, NT * 64, 0)
        s_v.wt(SE)
        s_lx.wt(VE, ld_g1)
        stq = [s_gx[0].n, s_gx[1].n]
        vmk = [s_v.n, s_v.n]
        for n in range(NT):
            par = n % 2
            xt = sxn[:, n * D:(n + 1) * D]
            tmp = (qb["t1"] if par == 0 else qb["t2"])[:, 0:D]
            Xst = qb["X"][:, par * 1024:par * 1024 + D]
            s_v.wt(SE, vmk[par])
            s_s.inc(SE.activation(tmp, xt, ACT.Identity,
                                  bias=lnc[:, 48 + n:49 + n],
                                  scale=lnc[:, 64 + n:65 + n]))
            s_gx[par].wt(VE, stq[par])
            s_s.wt(VE)
            s_v.inc(VE.tensor_tensor(Xst, tmp, g1rep[:], ALU.mult))
            vmk[par] = s_v.n
            s_v.wt(GE)
            s_gx[par].d(GE.dma_start(xn2_dd[n * 128:(n + 1) * 128, :], Xst))
            stq[par] = s_gx[par].n
        qb["_stx"] = stq
        quant_tiles(qb, [(sxn[:, n * D:(n + 1) * D], n * 64, D,
                          xq_dd[n * 128:(n + 1) * 128, :]) for n in range(NT)],
                    s_gd)

    ska = ExitStack()
    winT_bf = sbuf("winT_bf", [128, 8 * I3], BF16, ska)
    woutT_bf = sbuf("woutT_bf", [128, 12 * D], BF16, ska)
    with ExitStack() as sk:
        row1 = sbuf("row1", [1, I3], F32, sk)
        bcol_bf = sbuf("bcol_bf", [128, 8], BF16, sk)
        pbias = [sk.enter_context(nc.psum_tensor(f"pbias{i}", [1, 512], F32)) for i in range(6)]
        s_v.wt(GE)
        s_s.wt(GE)
        for kj in range(8):
            s_gw.d(GE.dma_start(winT_bf[:, kj * I3:(kj + 1) * I3],
                                winT_d[kj * 128:(kj + 1) * 128, :]))
        gw_win = s_gw.n
        s_gw.d(GE.dma_start(woutT_bf[:].rearrange("p (j d) -> p j d", d=D),
                            woutT_d[:, :].rearrange("(j p) d -> p j d", p=128)))
        s_v.inc(VE.tensor_copy(bcol_bf[:], bcol_t[:]))
        v_bc = s_v.n
        for kj in range(8):
            s_gw.wt(TE, gw_win)
            s_v.wt(TE, v_bc)
            for c in range(6):
                ins = TE.matmul(pbias[c][:], bcol_bf[:, kj:kj + 1],
                                winT_bf[:, kj * I3 + c * 512:kj * I3 + (c + 1) * 512],
                                start=(kj == 0), stop=(kj == 7))
            s_t.inc(ins)
        s_t.wt(VE)
        for c in range(6):
            s_v.inc(VE.tensor_copy(row1[0:1, c * 512:(c + 1) * 512], pbias[c][:]))
        s_v.wt(GE)
        s_gm.d(GE.dma_start(bias_dd[0:1, :], row1[:]))
        s_gm.wt(SY)
        with nc.allow_non_contiguous_dma(reason="tiny bias column reload"):
            s_lb.d(SY.dma_start(bias_col[:, 0:24],
                                bias_dd[0:1, :].rearrange("o (c p) -> p o c", p=128).squeeze(1)))
    ld_bias = s_lb.n

    with ExitStack() as sk:
        wq = sbuf("wq", [128, 4 * D], F32, sk)
        qb = make_qbufs(sk, "w")
        srcs = [(gslice_d, 1), (uslice_d, 1), (dslice_d, None)]
        s_s.wt(SY)
        s_t.wt(SY)
        s_g.wt(SY)
        with nc.allow_non_contiguous_dma(reason="gmax broadcast"):
            for wi in range(3):
                s_lr.d(SY.dma_start(C[:, 4 + wi:5 + wi],
                                    am8_out[0:1, wi:wi + 1].broadcast_to([128, 1])))
        s_lr.wt(VE)
        for wi in range(3):
            s_v.inc(VE.reciprocal(col(20 + wi), col(4 + wi)))
        vsync()
        for wi in range(3):
            s_v.inc(VE.tensor_scalar(col(7 + wi), col(20 + wi), 2688.0, None, ALU.mult))
        vsync()
        s_v.wt(SE)
        for wi in range(3):
            s_s.inc(SE.activation(col(10 + wi), col(7 + wi), ACT.Ln))
        s_v.inc(VE.tensor_tensor(col(20), col(0), col(7), ALU.mult))
        s_v.inc(VE.tensor_tensor(col(21), col(0), col(8), ALU.mult))
        s_v.inc(VE.tensor_tensor(col(22), col(1), col(9), ALU.mult))
        vsync()
        s_v.inc(VE.reciprocal(col(13), col(20)))
        s_v.inc(VE.reciprocal(col(14), col(21)))
        s_v.inc(VE.reciprocal(col(15), col(22)))
        vsync()
        dsts = [wsg_sl, wsu_sl, wsd_sl]
        for wi, (src_d, pat) in enumerate(srcs):
            s_v.wt(SY)
            s_s.wt(SY)
            if pat:
                s_lw.d(SY.dma_start(wq[:].rearrange("p (j d) -> p j d", d=D),
                                    src_d[:, :].rearrange("(j p) d -> p j d", p=128)))
            else:
                s_lw.d(SY.dma_start(wq[:], src_d[:, :]))
            s_lw.wt(VE)
            wb = wq[:].rearrange("p (b k) -> p b k", k=16)
            s_v.inc(VE.tensor_reduce(qb["amax"][:, 0:256], wb, AXX, ALU.max,
                                     apply_absolute_value=True))
            scale_chain(qb, 256, 7 + wi)
            tiles = []
            for tt in range(4):
                if pat:
                    dap = dsts[wi][tt * 128:(tt + 1) * 128, :]
                else:
                    dap = dsts[wi][:, tt * D:(tt + 1) * D]
                tiles.append((wq[:, tt * D:(tt + 1) * D], tt * 64, D, dap))
            quant_tiles(qb, tiles, s_gd)
        s_gx[0].wt(GE)
        s_gx[1].wt(GE)
        s_g.inc(GE.collective_compute("AllGather", ALU.bypass, rg,
                                      [wsg_sl[:, :]], [wsg_f[:, :]]))
        s_g.inc(GE.collective_compute("AllGather", ALU.bypass, rg,
                                      [wsu_sl[:, :]], [wsu_f[:, :]]))
        s_g.inc(GE.collective_compute("AllGather", ALU.bypass, rg,
                                      [wsd_sl[:, :]], [wsd_f[:, :]]))
    g_ws = s_g.n

    pb = [st.enter_context(nc.psum_tensor(f"pb{i}", [128, 512], F32)) for i in range(6)]
    ev = [(None, 0)] * 6

    with ExitStack() as sk:
        xn2T = sbuf("xn2T", [128, 8 * GT], BF16, sk)
        xssm = sbuf("xssm", [128, 12 * GT], F32, sk)
        tanhg = sbuf("tanhg", [128, 12 * GT], F32, sk)
        ybuf = sbuf("ybuf", [128, 12 * GT], F32, sk)
        gatedb = sbuf("gatedb", [128, 12 * GT], BF16, sk)
        htmp = sbuf("htmp", [128, 3 * GT], F32, sk)
        attn = sbuf("attn", [128, D], F32, sk)
        z1b = sbuf("z1b", [128, D], F32, sk)

        for qe in (SY, SE):
            s_v.wt(qe)
            s_gd.wt(qe)
            s_gw.wt(qe)
            s_gx[0].wt(qe)
            s_gx[1].wt(qe)
        s_s.wt(SY)
        for gg in range(NG):
            scm = s_sc if gg % 2 == 0 else s_s2
            s_t.wt(SY)
            for kj in range(8):
                scm.d(SY.dma_start(
                    xn2T[:, kj * GT:(kj + 1) * GT],
                    xn2_dd[gg * GT:(gg + 1) * GT, kj * 128:(kj + 1) * 128],
                    transpose=True))
            sc_x = scm.n
            s_lb.wt(VE, ld_bias)
            for oc in range(24):
                bank = oc % 2
                scm.wt(TE, sc_x)
                sm, cnt = ev[bank]
                if sm is not None:
                    sm.wt(TE, cnt)
                for kj in range(8):
                    ins = TE.matmul(pb[bank][:],
                                    winT_bf[:, kj * I3 + oc * 128:kj * I3 + (oc + 1) * 128],
                                    xn2T[:, kj * GT:(kj + 1) * GT],
                                    start=(kj == 0), stop=(kj == 7))
                s_t.inc(ins)
                s_t.wt(VE)
                dst = (xssm if oc < 12 else tanhg)
                j = oc if oc < 12 else oc - 12
                dd = dst[:, j * GT:(j + 1) * GT]
                s_v.inc(VE.tensor_scalar(dd, pb[bank][:], bias_col[:, oc:oc + 1], 5.0,
                                         ALU.add, ALU.min))
                s_v.inc(VE.tensor_scalar(dd, dd, -5.0, None, ALU.max))
                ev[bank] = (s_v, s_v.n)
            s_v.wt(SE)
            for j in range(12):
                s_s.inc(SE.activation(tanhg[:, j * GT:(j + 1) * GT],
                                      tanhg[:, j * GT:(j + 1) * GT], ACT.Tanh))
            pend = []
            for j in range(12):
                xs_j = xssm[:, j * GT:(j + 1) * GT]
                y_j = ybuf[:, j * GT:(j + 1) * GT]
                for d in range(DS):
                    cc = j * 8 + d
                    k3 = (j * 8 + d) % 3
                    ht = htmp[:, k3 * GT:(k3 + 1) * GT]
                    dk = decay_t[:, cc:cc + 1].broadcast_to([128, GT])
                    s_v.inc(VE.tensor_tensor_scan(ht, dk, xs_j,
                                                  carry_t[:, cc:cc + 1],
                                                  ALU.mult, ALU.add))
                    s_v.inc(VE.tensor_scalar(ht, ht, bpos_t[:, cc:cc + 1],
                                             bneg_t[:, cc:cc + 1], ALU.min, ALU.max))
                    pend.append((cc, k3))
                    if len(pend) > 2:
                        pc, pk = pend.pop(0)
                        s_v.inc(VE.tensor_copy(carry_t[:, pc:pc + 1],
                                               htmp[:, pk * GT + GT - 1:pk * GT + GT]))
                    if d == 0:
                        s_v.inc(VE.tensor_scalar(y_j, ht, wyc_t[:, cc:cc + 1],
                                                 None, ALU.mult))
                    else:
                        s_v.inc(VE.scalar_tensor_tensor(y_j, ht,
                                                        wyc_t[:, cc:cc + 1], y_j,
                                                        ALU.mult, ALU.add))
            vsync()
            for pc, pk in pend:
                s_v.inc(VE.tensor_copy(carry_t[:, pc:pc + 1],
                                       htmp[:, pk * GT + GT - 1:pk * GT + GT]))
            pend = []
            s_s.wt(VE)
            s_t.wt(VE)
            for j in range(12):
                y_j = ybuf[:, j * GT:(j + 1) * GT]
                s_v.inc(VE.tensor_scalar(y_j, y_j, 5.0, -5.0, ALU.min, ALU.max))
                s_v.inc(VE.tensor_tensor(gatedb[:, j * GT:(j + 1) * GT], y_j,
                                         tanhg[:, j * GT:(j + 1) * GT], ALU.mult))
            v_gate = s_v.n
            for tc4 in range(4):
                tglob = gg * 4 + tc4
                for oh in range(2):
                    bank = 2 + oh
                    s_v.wt(TE, v_gate)
                    sm, cnt = ev[bank]
                    if sm is not None:
                        sm.wt(TE, cnt)
                    for kj in range(12):
                        ins = TE.matmul(
                            pb[bank][:],
                            gatedb[:, kj * GT + tc4 * 128:kj * GT + (tc4 + 1) * 128],
                            woutT_bf[:, kj * D + oh * 512:kj * D + (oh + 1) * 512],
                            start=(kj == 0), stop=(kj == 11))
                    s_t.inc(ins)
                    s_t.wt(VE)
                    s_gd.wt(VE)
                    s_v.inc(VE.tensor_scalar(attn[:, oh * 512:(oh + 1) * 512],
                                             pb[bank][:], 5.0, -5.0, ALU.min, ALU.max))
                    ev[bank] = (s_v, s_v.n)
                ln_stats(attn, 20, 21)
                s_v.wt(SE)
                s_gd.wt(SE)
                s_s.inc(SE.activation(z1b[:], attn[:], ACT.Identity, bias=col(21),
                                      scale=col(20)))
                s_s.wt(VE)
                s_v.inc(VE.tensor_tensor(z1b[:], z1b[:], g2rep[:], ALU.mult))
                s_v.inc(VE.tensor_tensor(z1b[:], z1b[:], b2rep[:], ALU.add))
                s_v.wt(GE)
                s_gd.d(GE.dma_start(s_dd[tglob * 128:(tglob + 1) * 128, :], z1b[:]))
    ska.close()

    with ExitStack() as sk:
        xqT = sbuf("xqT", [128, 8 * T], BF16, sk)
        wgT = sbuf("wgT", [128, 2 * 8 * 512], BF16, sk)
        wuT = sbuf("wuT", [128, 2 * 8 * 512], BF16, sk)
        hidc = sbuf("hidc", [128, NT * 512], F32, sk)
        silu_g = sbuf("silu_g", [128, 2 * 512], F32, sk)
        qb = make_qbufs(sk, "h", w=512)

        for qe in (SY, SE):
            s_v.wt(qe)
            s_t.wt(qe)
            s_gd.wt(qe)
            s_gx[0].wt(qe)
            s_gx[1].wt(qe)
            s_g.wt(qe, g_ws)
        s_s.wt(SY)
        for kj in range(8):
            s_sc.d(SY.dma_start(
                xqT[:, kj * T:(kj + 1) * T],
                xq_dd[:, kj * 128:(kj + 1) * 128], transpose=True))

        def wload(ic):
            o = (ic % 2) * 8 * 512
            scm = s_sc if ic % 2 == 0 else s_s2
            for kj in range(8):
                scm.d(SY.dma_start(
                    wgT[:, o + kj * 512:o + (kj + 1) * 512],
                    wsg_f[ic * 512:(ic + 1) * 512, kj * 128:(kj + 1) * 128],
                    transpose=True))
                scm.d(SY.dma_start(
                    wuT[:, o + kj * 512:o + (kj + 1) * 512],
                    wsu_f[ic * 512:(ic + 1) * 512, kj * 128:(kj + 1) * 128],
                    transpose=True))
            return (scm, scm.n)
        sc_slot = [wload(0), None]
        t_mm = [0, 0]
        prev_tiles = None
        for ic in range(8):
            par_w = ic % 2
            o = par_w * 8 * 512
            if ic + 1 < 8:
                s_t.wt(SY, t_mm[(ic + 1) % 2])
                sc_slot[(ic + 1) % 2] = wload(ic + 1)
            sc_w = sc_slot[par_w]
            v_sg = [0, 0]
            pend = None
            for tc in range(NT):
                bg = tc % 2
                bu = 2 + tc % 2
                sc_w[0].wt(TE, sc_w[1])
                sm, cnt = ev[bg]
                if sm is not None:
                    sm.wt(TE, cnt)
                for kj in range(8):
                    ins = TE.matmul(pb[bg][:],
                                    xqT[:, kj * T + tc * 128:kj * T + (tc + 1) * 128],
                                    wgT[:, o + kj * 512:o + (kj + 1) * 512],
                                    start=(kj == 0), stop=(kj == 7))
                s_t.inc(ins)
                t_gate = s_t.n
                sm, cnt = ev[bu]
                if sm is not None:
                    sm.wt(TE, cnt)
                for kj in range(8):
                    ins = TE.matmul(pb[bu][:],
                                    xqT[:, kj * T + tc * 128:kj * T + (tc + 1) * 128],
                                    wuT[:, o + kj * 512:o + (kj + 1) * 512],
                                    start=(kj == 0), stop=(kj == 7))
                s_t.inc(ins)
                t_up = s_t.n
                if prev_tiles is not None:
                    src, boff, Fn, dst = prev_tiles[tc]
                    qb["_s1off"] = boff
                    cnt = quant_s1(qb, src, Fn, tc % 2)
                    if pend is not None:
                        quant_s2(qb, *pend)
                    pend = (boff, Fn, tc % 2, dst, s_gd, cnt)
                sg = silu_g[:, (tc % 2) * 512:(tc % 2) * 512 + 512]
                s_t.wt(SE, t_gate)
                s_v.wt(SE, v_sg[tc % 2])
                s_s.inc(SE.activation(sg, pb[bg][:], ACT.Silu, scale=col(13)))
                ev[bg] = (s_s, s_s.n)
                s_t.wt(VE, t_up)
                s_s.wt(VE)
                s_v.inc(VE.scalar_tensor_tensor(hidc[:, tc * 512:(tc + 1) * 512],
                                                pb[bu][:], col(14), sg,
                                                ALU.mult, ALU.mult))
                ev[bu] = (s_v, s_v.n)
                v_sg[tc % 2] = s_v.n
                hb = hidc[:, tc * 512:(tc + 1) * 512].rearrange("p (b k) -> p b k", k=16)
                s_v.inc(VE.tensor_reduce(qb["amax"][:, tc * 32:(tc + 1) * 32], hb, AXX,
                                         ALU.max, apply_absolute_value=True))
            t_mm[par_w] = s_t.n
            if pend is not None:
                quant_s2(qb, *pend)
            scale_chain(qb, NT * 32, 1)
            prev_tiles = [(hidc[:, tc * 512:(tc + 1) * 512], tc * 32, 512,
                           hq_dd[tc * 128:(tc + 1) * 128,
                                 ic * 512:(ic + 1) * 512])
                          for tc in range(NT)]
        quant_tiles(qb, prev_tiles, s_gd)

    with ExitStack() as sk:
        TH = T // 2
        hqTh = sbuf("hqTh", [128, 32 * TH], BF16, sk)
        wdT2 = sbuf("wdT2", [128, 32 * D], BF16, sk)
        gch = sbuf("gch", [128, 2 * D], F32, sk)
        xch = sbuf("xch", [128, 2 * D], F32, sk)
        sch = sbuf("sch", [128, 2 * D], F32, sk)

        for qe in (SY, SE):
            s_v.wt(qe)
            s_t.wt(qe)
            s_gd.wt(qe)
            s_gx[0].wt(qe)
            s_gx[1].wt(qe)
        s_s.wt(SY)
        v_xs = [0, 0]
        st_g = [0, 0]
        for th in range(2):
            scm = s_sc if th == 0 else s_s2
            if th == 0:
                for q in range(32):
                    scm.d(SY.dma_start(
                        wdT2[:, q * D:(q + 1) * D],
                        wsd_f[:, q * 128:(q + 1) * 128],
                        transpose=True))
                    scm.d(SY.dma_start(
                        hqTh[:, q * TH:(q + 1) * TH],
                        hq_dd[th * TH:(th + 1) * TH, q * 128:(q + 1) * 128],
                        transpose=True))
            else:
                s_t.wt(SY)
                for q in range(32):
                    scm.d(SY.dma_start(
                        hqTh[:, q * TH:(q + 1) * TH],
                        hq_dd[th * TH:(th + 1) * TH, q * 128:(q + 1) * 128],
                        transpose=True))
            sc_h = (scm, scm.n)
            for tci in range(8):
                tc = th * 8 + tci
                par = tc % 2
                s_v.wt(SY, v_xs[par])
                s_lf[par].d(SY.dma_start(xch[:, par * D:(par + 1) * D],
                                         x_d[tc * 128:(tc + 1) * 128, :]))
                s_lf[par].d(SY.dma_start(sch[:, par * D:(par + 1) * D],
                                         s_dd[tc * 128:(tc + 1) * 128, :]))
                ldn = s_lf[par].n
                t_dn = [0, 0]
                for oh in range(2):
                    bank = par * 2 + oh
                    sc_h[0].wt(TE, sc_h[1])
                    sm, cnt = ev[bank]
                    if sm is not None:
                        sm.wt(TE, cnt)
                    for q in range(32):
                        ins = TE.matmul(
                            pb[bank][:],
                            hqTh[:, q * TH + tci * 128:q * TH + (tci + 1) * 128],
                            wdT2[:, q * D + oh * 512:q * D + (oh + 1) * 512],
                            start=(q == 0), stop=(q == 31))
                    s_t.inc(ins)
                    t_dn[oh] = s_t.n
                g = gch[:, par * D:(par + 1) * D]
                s_st[par].wt(SE, st_g[par])
                for oh in range(2):
                    s_t.wt(SE, t_dn[oh])
                    s_s.inc(SE.activation(g[:, oh * 512:(oh + 1) * 512],
                                          pb[par * 2 + oh][:], ACT.Silu,
                                          scale=col(15)))
                    ev[par * 2 + oh] = (s_s, s_s.n)
                s_s.wt(VE)
                s_lf[par].wt(VE, ldn)
                s_v.inc(VE.tensor_tensor(g, g, sch[:, par * D:(par + 1) * D],
                                         ALU.mult))
                s_v.inc(VE.tensor_tensor(g, g, xch[:, par * D:(par + 1) * D],
                                         ALU.add))
                v_xs[par] = s_v.n
                s_v.wt(GE)
                s_st[par].d(GE.dma_start(out_d[tc * 128:(tc + 1) * 128, :], g))
                st_g[par] = s_st[par].n
        s_st[0].wt(GE)
        s_st[1].wt(GE)

    return nc


_NC = None


def kernel(L0_attn_x, L0_attn_W_in, L0_attn_W_out, L0_attn_A_log, L0_attn_B,
           L0_attn_C, L0_attn_ln_in_g, L0_attn_ln_in_b, L0_attn_ln_out_g,
           L0_attn_ln_out_b, L0_ffn_gate_w, L0_ffn_up_w, L0_ffn_down_w,
           L0_ffn_s_in, L0_ffn_s_in_down):
    global _NC
    x = np.ascontiguousarray(np.asarray(L0_attn_x, dtype=np.float32))
    winT = np.ascontiguousarray(np.asarray(L0_attn_W_in, dtype=np.float32).T)
    woutT = np.ascontiguousarray(np.asarray(L0_attn_W_out, dtype=np.float32).T)
    gw = np.ascontiguousarray(np.asarray(L0_ffn_gate_w, dtype=np.float32))
    uw = np.ascontiguousarray(np.asarray(L0_ffn_up_w, dtype=np.float32))
    dw = np.ascontiguousarray(np.asarray(L0_ffn_down_w, dtype=np.float32))
    alog = np.ascontiguousarray(np.asarray(L0_attn_A_log, dtype=np.float32))
    bmat = np.ascontiguousarray(np.asarray(L0_attn_B, dtype=np.float32))
    cmat = np.ascontiguousarray(np.asarray(L0_attn_C, dtype=np.float32))
    lng = np.ascontiguousarray(np.asarray(L0_attn_ln_in_g, dtype=np.float32).reshape(1, D))
    lnb = np.ascontiguousarray(np.asarray(L0_attn_ln_in_b, dtype=np.float32).reshape(1, D))
    lng2 = np.ascontiguousarray(np.asarray(L0_attn_ln_out_g, dtype=np.float32).reshape(1, D))
    lnb2 = np.ascontiguousarray(np.asarray(L0_attn_ln_out_b, dtype=np.float32).reshape(1, D))
    sin = np.ascontiguousarray(np.asarray(L0_ffn_s_in, dtype=np.float32).reshape(1, 1))
    sind = np.ascontiguousarray(np.asarray(L0_ffn_s_in_down, dtype=np.float32).reshape(1, 1))

    if _NC is None:
        _NC = build_nc()
    in_maps = []
    for c in range(NCORE):
        in_maps.append({
            "x": np.ascontiguousarray(x[c]),
            "winT": winT, "woutT": woutT,
            "gate_slice": np.ascontiguousarray(gw[c * 512:(c + 1) * 512]),
            "up_slice": np.ascontiguousarray(uw[c * 512:(c + 1) * 512]),
            "down_slice": np.ascontiguousarray(dw[c * 128:(c + 1) * 128]),
            "a_log": alog, "b_mat": bmat, "c_mat": cmat,
            "ln_in_g": lng, "ln_in_b": lnb, "ln_out_g": lng2, "ln_out_b": lnb2,
            "s_in": sin, "s_in_down": sind,
        })
    res = run_bass_kernel_spmd(_NC, in_maps, core_ids=list(range(NCORE)))
    return np.stack([np.asarray(r["out"], dtype=np.float32) for r in res.results])
